# version 02
# baseline (speedup 1.0000x reference)
import math
import contextlib
import numpy as np
import ml_dtypes
import concourse.bass as bass
import concourse.mybir as mybir
from concourse.bass_utils import run_bass_kernel_spmd

F32 = mybir.dt.float32
BF16 = mybir.dt.bfloat16
I32 = mybir.dt.int32
AF = mybir.ActivationFunctionType
ALU = mybir.AluOpType
AXX = mybir.AxisListType.X

D = 4096
T = 1024
NCH = 32
NH_A = 32
EPS_LN = 1e-5
EPS_RMS = 1e-6
CA_SHIFT = 20.0
ISQ_A = 1.0 / math.sqrt(128.0)
ISQ_B = 1.0 / math.sqrt(512.0)

DEBUG_STAGE = 0


class Sem:
    __slots__ = ("h", "n", "name")
    def __init__(self, h, name=""):
        self.h = h
        self.n = 0
        self.name = name


INTERP_SAFE = False


class KB:
    def __init__(self):
        self.st = {e: [] for e in ("tensor", "vector", "scalar", "gpsimd", "sync")}
        self.meta = {e: [] for e in ("tensor", "vector", "scalar", "gpsimd", "sync")}

    def emit(self, eng, fn, meta=None):
        self.st[eng].append(fn)
        self.meta[eng].append(meta or ("op",))

    def wait(self, eng, sem, val):
        if val > 0:
            self.emit(eng, lambda e, s=sem.h, v=val: e.wait_ge(s, v),
                      meta=("wait", sem.name, val))

    def check_feasible(self):
        pos = {e: 0 for e in self.meta}
        sems = {}
        progressed = True
        while progressed:
            progressed = False
            for e, ms in self.meta.items():
                while pos[e] < len(ms):
                    m = ms[pos[e]]
                    if m[0] == "wait" and sems.get(m[1], 0) < m[2]:
                        break
                    if m[0] == "inc":
                        sems[m[1]] = sems.get(m[1], 0) + m[2]
                    pos[e] += 1
                    progressed = True
        stuck = [(e, pos[e], self.meta[e][pos[e]])
                 for e in self.meta if pos[e] < len(self.meta[e])]
        return stuck or None

    def mm(self, out, lhsT, rhs, start, stop, sem=None):
        if sem is None:
            self.emit("tensor", lambda e, o=out, a=lhsT, b=rhs, s1=start, s2=stop:
                      e.matmul(o, a, b, start=s1, stop=s2))
            return None
        sem.n += 1
        self.emit("tensor", lambda e, o=out, a=lhsT, b=rhs, s1=start, s2=stop, s=sem.h:
                  e.matmul(o, a, b, start=s1, stop=s2).then_inc(s, 1),
                  meta=("inc", sem.name, 1))
        return sem.n

    def trans(self, out, in_, iden, sem=None):
        if sem is None:
            self.emit("tensor", lambda e, o=out, i=in_, d=iden: e.transpose(o, i, d))
            return None
        sem.n += 1
        self.emit("tensor", lambda e, o=out, i=in_, d=iden, s=sem.h:
                  e.transpose(o, i, d).then_inc(s, 1),
                  meta=("inc", sem.name, 1))
        return sem.n

    def dma(self, eng, out, in_, sem):
        if INTERP_SAFE and sem.n > 0:
            self.wait(eng, sem, sem.n)
        sem.n += 16
        self.emit(eng, lambda e, o=out, i=in_, s=sem.h:
                  e.dma_start(out=o, in_=i).then_inc(s, 16),
                  meta=("inc", sem.name, 16))
        return sem.n

    def act(self, out, in_, func, bias, scale, sem=None, accum=None):
        if sem is None:
            self.emit("scalar", lambda e, o=out, i=in_, f=func, b=bias, sc=scale, a=accum:
                      e.activation(o, i, f, bias=b, scale=sc, accum_out=a))
            return None
        sem.n += 1
        self.emit("scalar", lambda e, o=out, i=in_, f=func, b=bias, sc=scale, a=accum, s=sem.h:
                  e.activation(o, i, f, bias=b, scale=sc, accum_out=a).then_inc(s, 1),
                  meta=("inc", sem.name, 1))
        return sem.n

    def scopy(self, out, in_, sem=None, scale=1.0, bias=0.0):
        if sem is None:
            self.emit("scalar", lambda e, o=out, i=in_, b=bias, sc=scale:
                      e.activation(o, i, AF.Copy, bias=b, scale=sc))
            return None
        sem.n += 1
        self.emit("scalar", lambda e, o=out, i=in_, b=bias, sc=scale, s=sem.h:
                  e.activation(o, i, AF.Copy, bias=b, scale=sc).then_inc(s, 1),
                  meta=("inc", sem.name, 1))
        return sem.n

    def vop(self, name, *args, sem=None, **kw):
        if sem is None:
            self.emit("vector", lambda e, n=name, a=args, k=dict(kw): getattr(e, n)(*a, **k))
            return None
        sem.n += 1
        self.emit("vector", lambda e, n=name, a=args, k=dict(kw), s=sem.h:
                  getattr(e, n)(*a, **k).then_inc(s, 1),
                  meta=("inc", sem.name, 1))
        return sem.n

    def gop(self, name, *args, sem=None, **kw):
        if sem is None:
            self.emit("gpsimd", lambda e, n=name, a=args, k=dict(kw): getattr(e, n)(*a, **k))
            return None
        sem.n += 1
        self.emit("gpsimd", lambda e, n=name, a=args, k=dict(kw), s=sem.h:
                  getattr(e, n)(*a, **k).then_inc(s, 1),
                  meta=("inc", sem.name, 1))
        return sem.n

    def coll(self, kind, groups, ins, outs, sem):
        sem.n += 1
        self.emit("gpsimd", lambda e, kk=kind, g=groups, ii=ins, uu=outs, s=sem.h:
                  e.collective_compute(kk, ALU.bypass, replica_groups=g,
                                       ins=ii, outs=uu).then_inc(s),
                  meta=("inc", sem.name, 16))
        return sem.n


def build_nc():
    nc = bass.Bass("TRN2", target_bir_lowering=False)

    x0T = nc.dram_tensor("x0T", [D, T], F32, kind="ExternalInput").ap()
    Wq = nc.dram_tensor("Wq", [D, D], F32, kind="ExternalInput").ap()
    Wk = nc.dram_tensor("Wk", [D, D], F32, kind="ExternalInput").ap()
    Wv = nc.dram_tensor("Wv", [D, D], F32, kind="ExternalInput").ap()
    Wo = nc.dram_tensor("Wo", [D, D], F32, kind="ExternalInput").ap()
    bvecs = nc.dram_tensor("bvecs", [128, 6 * NCH + 12], F32, kind="ExternalInput").ap()
    gq_row = nc.dram_tensor("gq_row", [1, D], F32, kind="ExternalInput").ap()
    gk_row = nc.dram_tensor("gk_row", [1, D], F32, kind="ExternalInput").ap()
    qw2d = nc.dram_tensor("qw2d", [D, D], I32, kind="ExternalInput").ap()
    scl_r = nc.dram_tensor("scl_r", [NCH, 128, 32], F32, kind="ExternalInput").ap()
    zer_r = nc.dram_tensor("zer_r", [NCH, 128, 32], F32, kind="ExternalInput").ap()
    WqkqT = nc.dram_tensor("WqkqT", [D, 512], F32, kind="ExternalInput").ap()
    WqkkT = nc.dram_tensor("WqkkT", [D, 512], F32, kind="ExternalInput").ap()
    WvbT = nc.dram_tensor("WvbT", [D, 512], F32, kind="ExternalInput").ap()
    WobT = nc.dram_tensor("WobT", [512, D], F32, kind="ExternalInput").ap()
    idnf = nc.dram_tensor("idnf", [128, 128], F32, kind="ExternalInput").ap()

    yT = nc.dram_tensor("yT", [D, T], F32, kind="ExternalOutput").ap()

    scr1 = nc.dram_tensor("scr1", [D, T], F32).ap()
    scr2 = nc.dram_tensor("scr2", [D, T], F32).ap()
    wts = nc.dram_tensor("wts", [NCH * 128, D], BF16).ap()
    mxscr = nc.dram_tensor("mxscr", [8, 128], F32).ap()
    dnscr = nc.dram_tensor("dnscr", [1, 1024], F32).ap()

    ka_in = nc.dram_tensor("ka_in", [D, T], BF16)
    ka_out = nc.dram_tensor("ka_out", [4 * D, T], BF16)
    va_in = nc.dram_tensor("va_in", [NH_A * 8 * 128, 128], BF16)
    va_out = nc.dram_tensor("va_out", [4 * NH_A * 8 * 128, 128], BF16)
    kb_in = nc.dram_tensor("kb_in", [512, T], BF16)
    kb_out = nc.dram_tensor("kb_out", [4 * 512, T], BF16)
    vb_in = nc.dram_tensor("vb_in", [8 * T, 64], BF16)
    vb_out = nc.dram_tensor("vb_out", [4 * 8 * T, 64], BF16)
    groups = [[0, 1, 2, 3], [4, 5, 6, 7]]

    with contextlib.ExitStack() as stack:
        sbt = lambda name, shape, dt: stack.enter_context(nc.sbuf_tensor(name, shape, dt))
        R1 = sbt("R1", [128, NCH * T], BF16)
        R2 = sbt("R2", [128, NCH * T], BF16)
        WS = sbt("WS", [128, 8192], BF16)
        XT = sbt("XT", [128, 8192], BF16)
        WDF = sbt("WDF", [128, 1024], F32)
        STF = [sbt(f"STF{i}", [128, 512], F32) for i in range(2)]
        X0F = [sbt(f"X0F{i}", [128, 512], F32) for i in range(2)]
        BIAS = sbt("BIAS", [128, 6 * NCH + 12], F32)
        SCL = sbt("SCL", [128, 64], F32)
        F32R = sbt("F32R", [128, 3072], F32)
        RB = sbt("RB", [1, 5120], BF16)
        IDN = sbt("IDN", [128, 128], F32)
        ONEC = sbt("ONEC", [128, 1], BF16)
        ONERF = sbt("ONERF", [1, 128], BF16)
        ONEF32 = sbt("ONEF32", [1, 128], F32)
        NEGCA = sbt("NEGCA", [128, 1], F32)
        ZEROC = sbt("ZEROC", [128, 1], F32)
        EPSC = sbt("EPSC", [1, 2], F32)
        MXQ = sbt("MXQ", [128, 16], F32)
        MXT = sbt("MXT", [8, 128], F32)
        QTL = sbt("QTL", [65, 1024], BF16)

        Mrow = F32R[0:1, 0:1024]
        VArow = F32R[0:1, 1024:2048]
        TMProw = F32R[0:1, 2048:3072]
        RSTD = Mrow
        RCP = VArow
        DN0 = TMProw
        DN64 = F32R[64:65, 2048:3072]
        GAINS = RB[0:1, 0:4096]
        RBROW = RB[0:1, 4096:5120]

        pS0 = stack.enter_context(nc.psum_tensor("pS0", [128, 1024], F32))
        pS1 = stack.enter_context(nc.psum_tensor("pS1", [128, 1024], F32))
        pO = stack.enter_context(nc.psum_tensor("pO", [128, 1024], F32))
        pD = stack.enter_context(nc.psum_tensor("pD", [128, 1024], F32))

        mks = lambda name: Sem(stack.enter_context(nc.semaphore(name)), name)
        sW, sX, sST, sKV = mks("sW"), mks("sX"), mks("sST"), mks("sKV")
        sT, sV, sS, sG = mks("sT"), mks("sV"), mks("sS"), mks("sG")
        sC1, sC2, sC3, sC4 = mks("sC1"), mks("sC2"), mks("sC3"), mks("sC4")
        sY = mks("sY")
        sAB = mks("sAB")
        sKB = mks("sKB")
        sNR = mks("sNR")
        sNG = mks("sNG")
        sWA, sWB = mks("sWA"), mks("sWB")
        sR = [mks(f"sR{i}") for i in range(4)]

        kb = KB()

        def rch(region, ch):
            return region[:, ch * T:(ch + 1) * T]

        kb.gop("memset", ONEC[:, :], 1.0)
        kb.gop("memset", ONERF[:, :], 1.0)
        kb.gop("memset", ONEF32[:, :], 1.0)
        kb.gop("memset", NEGCA[:, :], -CA_SHIFT)
        kb.gop("memset", ZEROC[:, :], 0.0)
        kb.gop("memset", EPSC[0:1, 0:1], EPS_LN)
        kb.gop("memset", EPSC[0:1, 1:2], EPS_RMS)
        kb.gop("engine_nop", sem=sG)
        consts_g = sG.n
        kb.dma("sync", IDN[:, :], idnf[:, :], sX)
        kb.dma("sync", BIAS[:, :], bvecs[:, :], sX)
        consts_x = sX.n
        BQ = BIAS[:, 0 * NCH:1 * NCH]
        BK = BIAS[:, 1 * NCH:2 * NCH]
        BV = BIAS[:, 2 * NCH:3 * NCH]
        BO = BIAS[:, 3 * NCH:4 * NCH]
        FB = BIAS[:, 4 * NCH:5 * NCH]
        BOB = BIAS[:, 5 * NCH:6 * NCH]
        BQB = BIAS[:, 6 * NCH:6 * NCH + 4]
        BKB = BIAS[:, 6 * NCH + 4:6 * NCH + 8]
        BVB = BIAS[:, 6 * NCH + 8:6 * NCH + 12]

        xld = [kb.dma("gpsimd", rch(R1, ch), x0T[ch * 128:(ch + 1) * 128, :], sW)
               for ch in range(NCH)]
        kb.wait("vector", sG, consts_g)
        kb.wait("tensor", sG, consts_g)
        kb.wait("tensor", sX, consts_x)

        def stats_chunks(region, wait_ld=None):
            prev_mm = 0
            if wait_ld is not None:
                kb.wait("vector", sW, wait_ld[-1])
            for ch in range(NCH):
                kb.wait("vector", sT, prev_mm)
                vv = kb.vop("tensor_tensor", XT[:, 0:T], rch(region, ch),
                            rch(region, ch), op=ALU.mult, sem=sV)
                kb.wait("tensor", sV, vv)
                st, sp = ch == 0, ch == NCH - 1
                xc = rch(region, ch)
                kb.mm(pO[0:1, 0:512], ONEC[:, :], xc[:, 0:512], st, sp)
                kb.mm(pO[0:1, 512:1024], ONEC[:, :], xc[:, 512:1024], st, sp)
                kb.mm(pD[0:1, 0:512], ONEC[:, :], XT[:, 0:512], st, sp)
                prev_mm = kb.mm(pD[0:1, 512:1024], ONEC[:, :], XT[:, 512:1024], st, sp,
                                sem=sT)

        acc_free = {0: 0, 1: 0}
        stats_chunks(R1, wait_ld=xld)

        def ln_finalize():
            kb.wait("vector", sT, sT.n)
            va = kb.vop("tensor_scalar", Mrow, pO[0:1, :], 1.0 / D, None, op0=ALU.mult,
                        sem=sV)
            vb = kb.vop("tensor_scalar", VArow, pD[0:1, :], 1.0 / D, None, op0=ALU.mult,
                        sem=sV)
            kb.wait("vector", sV, vb)
            vc = kb.vop("tensor_tensor", TMProw, Mrow, Mrow, op=ALU.mult, sem=sV)
            kb.wait("vector", sV, vc)
            vv = kb.vop("tensor_tensor", VArow, VArow, TMProw, op=ALU.subtract, sem=sV)
            kb.wait("tensor", sV, sV.n)
            kb.mm(pS0[:, 0:512], ONEF32[0:1, :], Mrow[:, 0:512], True, True)
            tm = kb.mm(pS0[:, 512:1024], ONEF32[0:1, :], Mrow[:, 512:1024], True, True,
                       sem=sT)
            kb.wait("scalar", sV, vv)
            sl = kb.act(VArow, VArow, AF.Ln, EPSC[0:1, 0:1], 1.0, sem=sS)
            kb.wait("scalar", sS, sl)
            ss = kb.act(VArow, VArow, AF.Exp, ZEROC[0:1, 0:1], -0.5, sem=sS)
            kb.wait("scalar", sS, ss)
            kb.wait("scalar", sT, tm)
            sr = kb.scopy(RBROW, VArow, sem=sS)
            kb.wait("tensor", sS, sr)
            kb.mm(pS1[:, 0:512], ONERF[0:1, :], RBROW[:, 0:512], True, True)
            tt = kb.mm(pS1[:, 512:1024], ONERF[0:1, :], RBROW[:, 512:1024], True, True,
                       sem=sT)
            return tt

        def ln_apply(region, mr_ready):
            kb.wait("vector", sT, mr_ready)
            kb.wait("vector", sST, sST.n)
            kb.wait("vector", sY, sY.n)
            kb.wait("vector", sS, sS.n)
            for ch in range(NCH):
                xc = rch(region, ch)
                for tb in range(2):
                    hs = slice(tb * 512, (tb + 1) * 512)
                    ve = kb.vop("tensor_tensor", STF[tb][:, :], xc[:, hs],
                                pS0[:, hs], op=ALU.subtract, sem=sV)
                    kb.wait("vector", sV, ve)
                    vf = kb.vop("tensor_tensor", xc[:, hs], STF[tb][:, :],
                                pS1[:, hs], op=ALU.mult, sem=sV)
                    kb.wait("vector", sV, vf)
            m = kb.vop("engine_nop", sem=sV)
            acc_free[0] = m
            acc_free[1] = m
            return m

        def dump_region(region, marker_sem, marker):
            kb.wait("gpsimd", marker_sem, marker)
            for ch in range(NCH):
                kb.dma("gpsimd", yT[ch * 128:(ch + 1) * 128, :], rch(region, ch), sY)
            kb.wait("sync", sY, sY.n)

        if DEBUG_STAGE == "xb":
            mk = kb.vop("engine_nop", sem=sV)
            dump_region(R1, sV, mk)
            return _finish(nc, kb)
        ln1 = ln_apply(R1, ln_finalize())
        if DEBUG_STAGE == "ln1":
            dump_region(R1, sV, ln1)
            return _finish(nc, kb)

        def proj(W, n_in, n_out, rhs_ch, epi, rhs_ready=None):
            n_ic = n_in // 128
            n_oc = n_out // 128
            Wr = W.rearrange("(ic p) o -> p ic o", p=128)
            ldv, mmdone = {}, {}
            if rhs_ready:
                kb.wait("tensor", sV, rhs_ready)
            kb.wait("gpsimd", sT, sT.n)
            kb.wait("gpsimd", sST, sST.n)
            kb.wait("gpsimd", sAB, sAB.n)
            kb.wait("gpsimd", sNR, sNR.n)
            kb.wait("gpsimd", sNG, sNG.n)
            for oc in range(n_oc):
                slot = oc % 2
                wsem = sWA if slot == 0 else sWB
                if oc >= 2:
                    kb.wait("gpsimd", sT, mmdone[oc - 2])
                half = max(n_ic // 2, 1) if n_ic >= 8 else n_ic
                for g in range(n_ic // half):
                    kb.dma("gpsimd",
                           WS[:, slot * 4096 + g * half * 128:
                              slot * 4096 + (g + 1) * half * 128],
                           Wr[:, g * half:(g + 1) * half, oc * 128:(oc + 1) * 128], wsem)
                ldv[oc] = wsem.n
                kb.wait("tensor", wsem, ldv[oc])
                acc = pS0 if oc % 2 == 0 else pS1
                kb.wait("tensor", sV, acc_free[oc % 2])
                for tb in range(2):
                    for ic in range(n_ic):
                        kb.mm(acc[:, tb * 512:(tb + 1) * 512],
                              WS[:, slot * 4096 + ic * 128:slot * 4096 + (ic + 1) * 128],
                              rhs_ch(ic)[:, tb * 512:(tb + 1) * 512],
                              ic == 0, ic == n_ic - 1,
                              sem=(sT if (ic == n_ic - 1 and tb == 1) else None))
                mmdone[oc] = sT.n
                acc_free[oc % 2] = epi(oc, acc, sT.n)

        def mk_epi_raw(dstreg, bias_cols, stats):
            state = {"mm": 0}
            def epi(oc, acc, tA):
                kb.wait("vector", sT, tA)
                dst = rch(dstreg, oc)
                vv = kb.vop("tensor_scalar", dst, acc[:, :], bias_cols[:, oc:oc + 1],
                            None, op0=ALU.add, sem=sV)
                if not stats:
                    return vv
                kb.wait("vector", sT, state["mm"])
                kb.wait("vector", sV, vv)
                v2 = kb.vop("tensor_tensor", XT[:, 0:T], dst, dst, op=ALU.mult, sem=sV)
                kb.wait("tensor", sV, v2)
                st, sp = oc == 0, oc == NCH - 1
                kb.mm(pD[0:1, 0:512], ONEC[:, :], XT[:, 0:512], st, sp)
                state["mm"] = kb.mm(pD[0:1, 512:1024], ONEC[:, :], XT[:, 512:1024],
                                    st, sp, sem=sT)
                return v2
            return epi

        proj(Wk, D, D, lambda ic: rch(R1, ic), mk_epi_raw(R2, BK, True), rhs_ready=ln1)
        if DEBUG_STAGE == "kraw0":
            mk = kb.vop("engine_nop", sem=sV)
            dump_region(R2, sV, mk)
            return _finish(nc, kb)

        def rms_apply(grow_dram, out_cb):
            kb.wait("gpsimd", sT, sT.n)
            gld = kb.dma("gpsimd", GAINS, grow_dram[0:1, :], sNG)
            kb.wait("vector", sT, sT.n)
            vv = kb.vop("tensor_scalar", RSTD, pD[0:1, :], 1.0 / D, None,
                        op0=ALU.mult, sem=sV)
            kb.wait("scalar", sV, vv)
            s1_ = kb.act(RSTD, RSTD, AF.Ln, EPSC[0:1, 1:2], 1.0, sem=sS)
            kb.wait("scalar", sS, s1_)
            s2_ = kb.act(RSTD, RSTD, AF.Exp, ZEROC[0:1, 0:1], -0.5, sem=sS)
            kb.wait("scalar", sS, s2_)
            ss = kb.scopy(RBROW, RSTD, sem=sS)
            kb.wait("tensor", sS, ss)
            kb.wait("tensor", sNG, gld)
            kb.wait("tensor", sS, sS.n)
            gfree = {0: 0, 1: acc_free[0]}
            for oc in range(NCH):
                gb = oc % 2
                G = pO if gb == 0 else pS0
                kb.wait("tensor", sV, gfree[gb])
                kb.mm(G[:, 0:512], GAINS[0:1, oc * 128:(oc + 1) * 128], RBROW[:, 0:512],
                      True, True)
                tg = kb.mm(G[:, 512:1024], GAINS[0:1, oc * 128:(oc + 1) * 128],
                           RBROW[:, 512:1024], True, True, sem=sT)
                kb.wait("vector", sT, tg)
                gfree[gb] = out_cb(oc, G)
            acc_free[0] = max(acc_free[0], gfree[1])

        KH = [XT[:, 0:1024], XT[:, 1024:2048]]
        kexp = {}
        def k_out(oc, G):
            slot = oc % 2
            if oc >= 2:
                kb.wait("vector", sST, sST.n)
            vv = kb.vop("tensor_tensor", KH[slot], rch(R2, oc), G[:, :],
                        op=ALU.mult, sem=sV)
            kb.wait("sync", sV, vv)
            kexp[oc] = kb.dma("sync", ka_in.ap()[oc * 128:(oc + 1) * 128, :],
                              KH[slot], sST)
            return vv
        rms_apply(gk_row, k_out)
        kb.wait("gpsimd", sST, sST.n)
        kb.coll("AllGather", groups, [ka_in.ap().opt()], [ka_out.ap().opt()], sC1)
        kb.wait("gpsimd", sC1, 1)

        def v_epi_mk(dst_block):
            def epi(oc, acc, tA):
                kb.wait("vector", sT, tA)
                for tb in range(2):
                    vv = kb.vop("tensor_scalar", STF[tb][:, :],
                                acc[:, tb * 512:(tb + 1) * 512], BV[:, oc:oc + 1],
                                None, op0=ALU.add, sem=sV)
                    kb.wait("tensor", sV, vv)
                    kb.wait("tensor", sS, sS.n)
                    for j in range(4):
                        tc = tb * 4 + j
                        kb.trans(pO[:, tc * 128:(tc + 1) * 128], STF[tb][:, j * 128:(j + 1) * 128],
                                 IDN[:, :], sem=(sT if j == 3 else None))
                    kb.wait("scalar", sT, sT.n)
                    for j in range(4):
                        tc = tb * 4 + j
                        kb.scopy(dst_block(tc)[:, oc * 128:(oc + 1) * 128],
                                 pO[:, tc * 128:(tc + 1) * 128],
                                 sem=(sS if j == 3 else None))
                return sV.n
            return epi
        proj(Wv, D, D, lambda ic: rch(R1, ic),
             v_epi_mk(lambda tc: R2[:, tc * 4096:(tc + 1) * 4096]))
        va_view = va_in.ap().rearrange("(h tc t) d -> tc t h d", h=NH_A, tc=8)
        kb.wait("sync", sS, sS.n)
        for tc in range(8):
            kb.dma("sync", va_view[tc],
                   R2[:, tc * 4096:(tc + 1) * 4096].rearrange("p (h d) -> p h d", d=128),
                   sST)
        kb.wait("gpsimd", sST, sST.n)
        kb.coll("AllGather", groups, [va_in.ap().opt()], [va_out.ap().opt()], sC2)
        kb.wait("gpsimd", sC2, 1)

        if DEBUG_STAGE == "qbarrier":
            kb.wait("gpsimd", sC1, 1)
            kb.wait("gpsimd", sC2, 1)
            kb.wait("gpsimd", sST, sST.n)
        if DEBUG_STAGE == "qcoll":
            kb.wait("gpsimd", sC1, 1)
            kb.wait("gpsimd", sC2, 1)
        if DEBUG_STAGE == "qexp":
            kb.wait("gpsimd", sST, sST.n)
        proj(Wq, D, D, lambda ic: rch(R1, ic), mk_epi_raw(R2, BQ, True))
        if DEBUG_STAGE == "xhat_after":
            mk = kb.vop("engine_nop", sem=sV)
            dump_region(R1, sV, mk)
            return _finish(nc, kb)
        def q_out(oc, G):
            return kb.vop("tensor_tensor", rch(R1, oc), rch(R2, oc), G[:, :],
                          op=ALU.mult, sem=sV)
        rms_apply(gq_row, q_out)
        qhat_done = sV.n
        if DEBUG_STAGE in ("qbarrier", "qcoll", "qexp"):
            dump_region(R2, sV, qhat_done)
            return _finish(nc, kb)
        if DEBUG_STAGE == "qhat":
            dump_region(R1, sV, qhat_done)
            return _finish(nc, kb)
        if DEBUG_STAGE == "kraw":
            dump_region(R2, sV, qhat_done)
            return _finish(nc, kb)

        kb.wait("scalar", sC1, 1)
        kb.wait("scalar", sC2, 1)
        kb.wait("scalar", sT, sT.n)
        ka_out_ap = ka_out.ap()
        va_view_o = va_out.ap().rearrange("(r h tc t) d -> r h t tc d",
                                          r=4, h=NH_A, tc=8)
        seq = [(h, r) for h in range(NH_A) for r in range(4)]
        ldmark = {}
        uses = {}
        def load_rank(i):
            h, r = seq[i]
            s = i % 4
            kb.dma("scalar", WS[:, s * 2048:s * 2048 + 1024],
                   ka_out_ap[r * D + h * 128:r * D + (h + 1) * 128, :], sR[s])
            ldmark[i] = kb.dma(
                "scalar",
                WS[:, s * 2048 + 1024:s * 2048 + 2048].rearrange("p (tc d) -> p tc d", d=128),
                va_view_o[r, h], sR[s])
        load_rank(0)
        load_rank(1)
        kb.wait("tensor", sV, qhat_done)
        head_tail_v = qhat_done
        head_tail_s = 0
        prev_exp_last = 0
        expc, pvc = {}, {}
        for i, (h, r) in enumerate(seq):
            s = i % 4
            if i + 2 < len(seq):
                if i >= 2:
                    kb.wait("scalar", sT, uses[i - 2])
                load_rank(i + 2)
            kbuf = WS[:, s * 2048:s * 2048 + 1024]
            vbuf = WS[:, s * 2048 + 1024:s * 2048 + 2048]
            kb.wait("tensor", sR[s], ldmark[i])
            for c8 in range(8):
                c = r * 8 + c8
                Sp = pS0 if c % 2 == 0 else pS1
                if c >= 2:
                    kb.wait("tensor", sS, expc[c - 2])
                elif h > 0:
                    kb.wait("tensor", sS, prev_exp_last)
                    kb.wait("tensor", sV, head_tail_v)
                kb.mm(Sp[:, 0:512], kbuf[:, c8 * 128:(c8 + 1) * 128],
                      rch(R1, h)[:, 0:512], True, True)
                ts = kb.mm(Sp[:, 512:1024], kbuf[:, c8 * 128:(c8 + 1) * 128],
                           rch(R1, h)[:, 512:1024], True, True, sem=sT)
                kb.wait("scalar", sT, ts)
                if c >= 3:
                    kb.wait("scalar", sT, pvc[c - 3])
                pbuf = XT[:, (c % 3) * 1024:(c % 3 + 1) * 1024]
                expc[c] = kb.act(pbuf, Sp[:, :], AF.Exp, NEGCA[:, :], ISQ_A, sem=sS)
                kb.wait("tensor", sS, expc[c])
                st, sp = c == 0, c == NCH - 1
                if st:
                    kb.wait("tensor", sS, head_tail_s)
                    kb.wait("tensor", sV, head_tail_v)
                kb.mm(pO[:, 0:512], vbuf[:, c8 * 128:(c8 + 1) * 128], pbuf[:, 0:512],
                      st, sp)
                kb.mm(pO[:, 512:1024], vbuf[:, c8 * 128:(c8 + 1) * 128],
                      pbuf[:, 512:1024], st, sp)
                kb.mm(pD[0:1, 0:512], ONEC[:, :], pbuf[:, 0:512], st, sp)
                pvc[c] = kb.mm(pD[0:1, 512:1024], ONEC[:, :], pbuf[:, 512:1024],
                               st, sp, sem=sT)
            uses[i] = pvc[r * 8 + 7]
            if r < 3:
                continue
            prev_exp_last = expc[NCH - 1]
            kb.wait("vector", sT, pvc[NCH - 1])
            vr = kb.vop("reciprocal", RCP, pD[0:1, :], sem=sV)
            kb.wait("scalar", sV, vr)
            kb.wait("scalar", sT, sT.n)
            src_rb = kb.scopy(RBROW, RCP, sem=sS)
            kb.wait("tensor", sS, src_rb)
            kb.wait("tensor", sS, expc[NCH - 2])
            kb.mm(pS0[:, 0:512], ONERF[0:1, :], RBROW[:, 0:512], True, True)
            tb1 = kb.mm(pS0[:, 512:1024], ONERF[0:1, :], RBROW[:, 512:1024], True, True,
                        sem=sT)
            kb.wait("scalar", sT, tb1)
            kb.wait("scalar", sV, head_tail_v)
            head_tail_s = kb.scopy(XT[:, 3 * 1024:4 * 1024], pO[:, :], sem=sS)
            kb.wait("vector", sS, head_tail_s)
            head_tail_v = kb.vop("tensor_tensor", rch(R2, h), XT[:, 3 * 1024:4 * 1024],
                                 pS0[:, :], op=ALU.mult, sem=sV)
        attn_done = sV.n
        acc_free[0] = head_tail_v
        acc_free[1] = head_tail_v
        if DEBUG_STAGE == "attn":
            dump_region(R2, sV, attn_done)
            return _finish(nc, kb)

        resid_ld = {}
        stf_free = [0, 0]
        x0f_free = [0, 0]
        stf_scpy = [0, 0]

        def mk_res_epi(src_ap, dst_ap, dst_reg, bias_cols, sto):
            def epi(oc, acc, tA):
                if oc == 0:
                    resid_ld.clear()
                    stf_free[0] = stf_free[1] = 0
                    kb.wait("vector", sST, sST.n)
                    kb.wait("vector", sY, sY.n)
                    kb.wait("vector", sS, sS.n)
                    stf_scpy[0] = stf_scpy[1] = 0
                    kb.wait("sync", sV, max(x0f_free))
                    kb.wait("sync", sST, sST.n)
                kb.wait("vector", sT, tA)
                for tb in range(2):
                    kb.wait("sync", sV, x0f_free[tb])
                    resid_ld[(oc, tb)] = kb.dma(
                        "sync", X0F[tb][:, :],
                        src_ap[oc * 128:(oc + 1) * 128, tb * 512:(tb + 1) * 512], sX)
                for tb in range(2):
                    kb.wait("vector", sX, resid_ld[(oc, 1)])
                    kb.wait("vector", sto, sto.n)
                    kb.wait("vector", sS, sS.n)
                    vv = kb.vop("scalar_tensor_tensor", STF[tb][:, :],
                                acc[:, tb * 512:(tb + 1) * 512],
                                bias_cols[:, oc:oc + 1], X0F[tb][:, :],
                                op0=ALU.add, op1=ALU.add, sem=sV)
                    x0f_free[tb] = vv
                    kb.wait("sync", sV, vv)
                    stf_free[tb] = kb.dma(
                        "sync", dst_ap[oc * 128:(oc + 1) * 128, tb * 512:(tb + 1) * 512],
                        STF[tb][:, :], sto)
                    if dst_reg is not None:
                        kb.wait("scalar", sV, vv)
                        stf_scpy[tb] = kb.scopy(
                            rch(dst_reg, oc)[:, tb * 512:(tb + 1) * 512],
                            STF[tb][:, :], sem=sS)
                return x0f_free[1]
            return epi

        to_y = DEBUG_STAGE == 1
        proj(Wo, D, D, lambda ic: rch(R2, ic),
             mk_res_epi(x0T, yT if to_y else scr1, None if to_y else R1, BO,
                        sY if to_y else sST),
             rhs_ready=attn_done)

        def ffn(first, rhs_reg, out_reg, src_ap, dst_ap, sto, rhs_ready):
            qwr = qw2d.rearrange("(oc p) i -> oc p i", p=128)
            epi = mk_res_epi(src_ap, dst_ap, out_reg, FB, sto)
            deq_used, mmdone, wt_st, wtld = {}, {}, {}, {}
            trdone, cpd = {}, {}
            if rhs_ready:
                kb.wait("tensor", sV, rhs_ready)
            kb.wait("gpsimd", sT, sT.n)
            kb.wait("gpsimd", sST, sST.n)
            kb.wait("scalar", sT, sT.n)
            kb.wait("scalar", sST, sST.n)
            for oc in range(NCH):
                if first:
                    if oc >= 1:
                        kb.wait("gpsimd", sV, deq_used[oc - 1])
                    kb.dma("gpsimd", WS[:, 0:2048], qwr[oc, :, 0:2048], sW)
                    kb.dma("gpsimd", WS[:, 2048:4096], qwr[oc, :, 2048:4096], sW)
                    kb.dma("gpsimd", SCL[:, 0:32], scl_r[oc], sW)
                    qldv = kb.dma("gpsimd", SCL[:, 32:64], zer_r[oc], sW)
                    kb.wait("vector", sW, qldv)
                    for w in range(8):
                        wh = (w % 2) * 512
                        g_t = trdone.get((oc, w - 2)) if w >= 2 else trdone.get((oc - 1, 6 + w))
                        if g_t:
                            kb.wait("vector", sT, g_t)
                        for j in range(4):
                            g = w * 4 + j
                            kb.vop("tensor_scalar", WDF[:, wh + j * 128:wh + (j + 1) * 128],
                                   WS[:, g * 128:(g + 1) * 128],
                                   SCL[:, g:g + 1], SCL[:, 32 + g:32 + g + 1],
                                   op0=ALU.mult, op1=ALU.add,
                                   sem=(sV if j == 3 else None))
                        dq = sV.n
                        kb.wait("tensor", sV, dq)
                        po_h = (w % 2) * 512
                        g_c = cpd.get((oc, w - 2)) if w >= 2 else cpd.get((oc - 1, 6 + w))
                        if g_c:
                            kb.wait("tensor", sS, g_c)
                        if oc >= 1 and w == 0:
                            kb.wait("scalar", sST, wt_st.get(oc - 1, 0))
                        for j in range(4):
                            kb.trans(pO[:, po_h + j * 128:po_h + (j + 1) * 128],
                                     WDF[:, wh + j * 128:wh + (j + 1) * 128],
                                     IDN[:, :], sem=(sT if j == 3 else None))
                        trdone[(oc, w)] = sT.n
                        kb.wait("scalar", sT, trdone[(oc, w)])
                        cpd[(oc, w)] = kb.scopy(WS[:, 4096 + w * 512:4096 + (w + 1) * 512],
                                                pO[:, po_h:po_h + 512], sem=sS)
                    deq_used[oc] = sV.n
                    wcopy = sS.n
                    wt = WS[:, 4096:8192]
                    kb.wait("sync", sS, wcopy)
                    wt_st[oc] = kb.dma("sync", wts[oc * 128:(oc + 1) * 128, :], wt, sST)
                    kb.wait("tensor", sS, wcopy)
                else:
                    slot = oc % 2
                    wsem = sKV if slot == 0 else sKB
                    if oc == 0:
                        kb.wait("scalar", sST, sST.n)
                    if oc >= 2:
                        kb.wait("scalar", sT, mmdone[oc - 2])
                    kb.dma("scalar", WS[:, slot * 4096:slot * 4096 + 4096],
                           wts[oc * 128:(oc + 1) * 128, :], wsem)
                    wtld[oc] = wsem.n
                    kb.wait("tensor", wsem, wtld[oc])
                    wt = WS[:, slot * 4096:slot * 4096 + 4096]
                acc = pS0 if oc % 2 == 0 else pS1
                kb.wait("tensor", sV, acc_free[oc % 2])
                for tb in range(2):
                    for ic in range(NCH):
                        kb.mm(acc[:, tb * 512:(tb + 1) * 512],
                              wt[:, ic * 128:(ic + 1) * 128],
                              rch(rhs_reg, ic)[:, tb * 512:(tb + 1) * 512],
                              ic == 0, ic == NCH - 1,
                              sem=(sT if (ic == NCH - 1 and tb == 1) else None))
                mmdone[oc] = sT.n
                acc_free[oc % 2] = epi(oc, acc, sT.n)

        if DEBUG_STAGE != 1:
            kb.wait("vector", sS, sS.n)
            if DEBUG_STAGE == "x1b":
                mk = kb.vop("engine_nop", sem=sV)
                kb.wait("gpsimd", sS, sS.n)
                dump_region(R1, sV, mk)
                return _finish(nc, kb)
            stats_chunks(R1)
            ln2 = ln_apply(R1, ln_finalize())
            if DEBUG_STAGE == "ln2":
                dump_region(R1, sV, ln2)
                return _finish(nc, kb)
            to_y = DEBUG_STAGE == 2
            ffn(True, R1, None if to_y else R2, scr1, yT if to_y else scr2,
                sY if to_y else sST, ln2)

        if DEBUG_STAGE in (0, 3, "abo"):
            kb.wait("vector", sS, sS.n)
            stats_chunks(R2)
            ln3 = ln_apply(R2, ln_finalize())

            QB = lambda ch: XT[:, 4096 + ch * T:4096 + (ch + 1) * T]
            ABO = lambda ch: rch(R2, ch)
            VBT = lambda tc: R1[:, tc * 512:(tc + 1) * 512]

            def qb_epi(oc, acc, tA):
                kb.wait("vector", sT, tA)
                return kb.vop("tensor_scalar", QB(oc), acc[:, :], BQB[:, oc:oc + 1],
                              None, op0=ALU.add, sem=sV)
            proj(WqkqT, D, 512, lambda ic: rch(R2, ic), qb_epi, rhs_ready=ln3)

            kexp_b = {}
            def kbp_epi(oc, acc, tA):
                slot = oc % 2
                kb.wait("vector", sT, tA)
                if oc >= 2:
                    kb.wait("vector", sST, sST.n)
                vv = kb.vop("tensor_scalar", KH[slot], acc[:, :],
                            BKB[:, oc:oc + 1], None, op0=ALU.add, sem=sV)
                kb.wait("sync", sV, vv)
                kexp_b[oc] = kb.dma("sync", kb_in.ap()[oc * 128:(oc + 1) * 128, :],
                                    KH[slot], sST)
                return vv
            proj(WqkkT, D, 512, lambda ic: rch(R2, ic), kbp_epi)
            kb.wait("gpsimd", sST, sST.n)
            kb.coll("AllGather", groups, [kb_in.ap().opt()], [kb_out.ap().opt()], sC3)
            kb.wait("gpsimd", sC3, 1)

            def vbp_epi(oc, acc, tA):
                kb.wait("vector", sT, tA)
                if oc < 2:
                    kb.wait("vector", sST, sST.n)
                    kb.wait("vector", sS, sS.n)
                for tb in range(2):
                    vv = kb.vop("tensor_scalar", STF[tb][:, :],
                                acc[:, tb * 512:(tb + 1) * 512], BVB[:, oc:oc + 1],
                                None, op0=ALU.add, sem=sV)
                    kb.wait("tensor", sV, vv)
                    kb.wait("tensor", sS, sS.n)
                    for j in range(4):
                        tc = tb * 4 + j
                        kb.trans(pO[:, tc * 128:(tc + 1) * 128],
                                 STF[tb][:, j * 128:(j + 1) * 128],
                                 IDN[:, :], sem=(sT if j == 3 else None))
                    kb.wait("scalar", sT, sT.n)
                    for j in range(4):
                        tc = tb * 4 + j
                        kb.scopy(VBT(tc)[:, oc * 128:(oc + 1) * 128],
                                 pO[:, tc * 128:(tc + 1) * 128],
                                 sem=(sS if j == 3 else None))
                return sV.n
            proj(WvbT, D, 512, lambda ic: rch(R2, ic), vbp_epi)
            vb_view = vb_in.ap().rearrange("(h tc t) d -> tc t h d", h=8, tc=8)
            kb.wait("sync", sS, sS.n)
            for tc in range(8):
                kb.dma("sync", vb_view[tc],
                       VBT(tc).rearrange("p (h d) -> p h d", d=64), sST)
            kb.wait("gpsimd", sST, sST.n)
            kb.coll("AllGather", groups, [vb_in.ap().opt()], [vb_out.ap().opt()], sC4)
            kb.wait("gpsimd", sC4, 1)

            kb.wait("scalar", sC3, 1)
            kb.wait("scalar", sC4, 1)
            kb.wait("scalar", sT, sT.n)
            kb.wait("gpsimd", sT, sT.n)
            kbo_ap = kb_out.ap()
            vbo_view = vb_out.ap().rearrange("(r h tc t) d -> r h t tc d",
                                             r=4, h=8, tc=8)
            kb.gop("memset", WS[64:65, 0:4096], 1.0)
            for s in range(2):
                base = 4096 + s * 520
                kb.gop("memset", WS[:, base + 64:base + 520:65], 1.0)
            kb.gop("engine_nop", sem=sG)
            kb.wait("scalar", sG, sG.n)

            ktb = WS[:, 0:4096]
            kldb = {}
            vldb = {}
            vsem = {}
            vuse = {}
            def load_kt(h, guard):
                kb.wait("scalar", sT, guard)
                for r in range(4):
                    kb.dma("scalar", WS[0:64, r * 1024:(r + 1) * 1024],
                           kbo_ap[r * 512 + h * 64:r * 512 + (h + 1) * 64, :], sKB)
                kldb[h] = sKB.n
            def load_vt(h, r, guard):
                s = (h * 4 + r) % 2
                kb.wait("scalar", sT, guard)
                base = 4096 + s * 520
                dst = WS[:, base:base + 520].rearrange("p (tc d) -> p tc d", d=65)[:, :, 0:64]
                vldb[(h, r)] = kb.dma("scalar", dst, vbo_view[r, h], sR[s])
                vsem[(h, r)] = s

            load_kt(0, 0)
            abo_marks = []
            qb_ready = sV.n
            mx_hist = {"store": 0, "qtl": 0}
            mxg = {0: acc_free[0], 1: acc_free[0], 2: acc_free[1], 3: acc_free[1]}
            ht_v = sV.n
            ht_s = 0
            prev_ts_b = 0
            last_vuse = {0: 0, 1: 0}
            for h in range(8):
                kb.wait("tensor", sKB, kldb[h])
                load_vt(h, 0, max(prev_ts_b, last_vuse[(h * 4) % 2]))
                load_vt(h, 1, max(prev_ts_b, last_vuse[(h * 4 + 1) % 2]))
                qrows = QB(h // 2)[64 * (h % 2):64 * (h % 2) + 64, :]
                kb.wait("sync", sT, prev_ts_b)
                kb.wait("sync", sV, qb_ready)
                qcp = kb.dma("sync", QTL[0:64, :], qrows, sNR)
                REG = [pS0[:, 0:512], pS0[:, 512:1024], pS1[:, 0:512], pS1[:, 512:1024]]
                kb.wait("tensor", sNR, qcp)
                for r in range(4):
                    for qt in range(8):
                        for k2 in range(2):
                            i4 = (r * 16 + qt * 2 + k2) % 4
                            kb.wait("tensor", sV, mxg[i4])
                            tsx = kb.mm(REG[i4], QTL[0:64, qt * 128:(qt + 1) * 128],
                                        ktb[0:64, r * 1024 + k2 * 512:r * 1024 + (k2 + 1) * 512],
                                        True, True, sem=sT)
                            kb.wait("vector", sT, tsx)
                            col = 8 + k2
                            kb.vop("reduce_max", MXQ[:, col:col + 1], REG[i4], axis=AXX,
                                   sem=sV)
                            mxg[i4] = sV.n
                            kb.wait("vector", sV, mxg[i4])
                        if r == 0:
                            vm = kb.vop("tensor_tensor", MXQ[:, qt:qt + 1], MXQ[:, 8:9],
                                        MXQ[:, 9:10], op=ALU.max, sem=sV)
                        else:
                            vm = kb.vop("tensor_tensor", MXQ[:, qt:qt + 1],
                                        MXQ[:, qt:qt + 1], MXQ[:, 8:9], op=ALU.max,
                                        sem=sV)
                            kb.wait("vector", sV, vm)
                            vm = kb.vop("tensor_tensor", MXQ[:, qt:qt + 1],
                                        MXQ[:, qt:qt + 1], MXQ[:, 9:10], op=ALU.max,
                                        sem=sV)
                        kb.wait("vector", sV, vm)
                vn = kb.vop("tensor_scalar", MXQ[:, 0:8], MXQ[:, 0:8], -1.0, None,
                            op0=ALU.mult, sem=sV)
                kb.wait("vector", sV, vn)
                mx_last = kb.vop("engine_nop", sem=sV)
                kb.wait("tensor", sV, mx_last)
                kb.wait("tensor", sV, ht_v)
                tmx = kb.trans(pD[0:8, 0:128], MXQ[:, 0:8], IDN[:, :], sem=sT)
                kb.wait("scalar", sT, tmx)
                kb.wait("scalar", sNR, mx_hist["store"])
                smx = kb.scopy(MXT[:, :], pD[0:8, 0:128], sem=sS)
                kb.wait("sync", sS, smx)
                mx_hist["store"] = kb.dma("sync", mxscr[:, :], MXT[:, :], sNR)
                kb.wait("gpsimd", sNR, mx_hist["store"])
                kb.wait("gpsimd", sT, prev_ts_b)
                qtl_s = kb.dma("gpsimd", QTL[64:65, :],
                               mxscr.rearrange("a b -> (a b)")[None, :], sNG)
                kb.wait("tensor", sNG, qtl_s)
                kb.wait("tensor", sV, mx_last)
                expc, pvc = {}, {}
                for c in range(32):
                    r, c8 = c // 8, c % 8
                    if c8 == 0 and 1 <= r < 3:
                        load_vt(h, r + 1, vuse[(h, r - 1)])
                    s = (h * 4 + r) % 2
                    vtb = WS[:, 4096 + s * 520:4096 + (s + 1) * 520]
                    if c8 == 0:
                        kb.wait("tensor", sR[vsem[(h, r)]], vldb[(h, r)])
                    Sp = pS0 if c % 2 == 0 else pS1
                    if c >= 2:
                        kb.wait("tensor", sS, expc[c - 2])
                    kb.mm(Sp[:, 0:512], ktb[0:65, c * 128:(c + 1) * 128],
                          QTL[0:65, 0:512], True, True)
                    ts = kb.mm(Sp[:, 512:1024], ktb[0:65, c * 128:(c + 1) * 128],
                               QTL[0:65, 512:1024], True, True, sem=sT)
                    kb.wait("scalar", sT, ts)
                    if c >= 3:
                        kb.wait("scalar", sT, pvc[c - 3])
                    pbuf = XT[:, (c % 3) * 1024:(c % 3 + 1) * 1024]
                    if c == 2:
                        kb.wait("scalar", sV, ht_v)
                    expc[c] = kb.act(pbuf, Sp[:, :], AF.Exp, ZEROC[:, :], 1.0, sem=sS)
                    kb.wait("tensor", sS, expc[c])
                    st, sp = c == 0, c == 31
                    if st:
                        kb.wait("tensor", sS, ht_s)
                    kb.mm(pO[0:65, 0:512], vtb[:, c8 * 65:(c8 + 1) * 65],
                          pbuf[:, 0:512], st, sp)
                    pvc[c] = kb.mm(pO[0:65, 512:1024], vtb[:, c8 * 65:(c8 + 1) * 65],
                                   pbuf[:, 512:1024], st, sp, sem=sT)
                    if c8 == 7:
                        vuse[(h, r)] = pvc[c]
                        last_vuse[(h * 4 + r) % 2] = pvc[c]
                prev_ts_b = pvc[31]
                if h + 1 < 8:
                    load_kt(h + 1, prev_ts_b)
                kb.wait("scalar", sT, pvc[31])
                dcp = kb.scopy(DN64, pO[64:65, :], sem=sS)
                kb.wait("sync", sS, dcp)
                dst1 = kb.dma("sync", dnscr[0:1, :], DN64, sNR)
                kb.wait("sync", sNR, dst1)
                dld = kb.dma("sync", DN0, dnscr[0:1, :], sNR)
                kb.wait("vector", sNR, dld)
                vr = kb.vop("reciprocal", RCP, DN0, sem=sV)
                kb.wait("scalar", sV, vr)
                kb.wait("scalar", sT, sT.n)
                srb = kb.scopy(RBROW, RCP, scale=ISQ_B, sem=sS)
                kb.wait("tensor", sS, srb)
                kb.mm(pD[0:64, 0:512], ONERF[0:1, 0:64], RBROW[:, 0:512], True, True)
                tbb = kb.mm(pD[0:64, 512:1024], ONERF[0:1, 0:64], RBROW[:, 512:1024],
                            True, True, sem=sT)
                kb.wait("scalar", sT, tbb)
                kb.wait("scalar", sV, ht_v)
                ht_s = kb.scopy(XT[0:64, 2 * 1024:3 * 1024], pO[0:64, :], sem=sS)
                kb.wait("vector", sS, ht_s)
                kb.wait("vector", sAB, abo_marks[-1] if abo_marks else 0)
                ht_v = kb.vop("tensor_tensor", XT[0:64, 3 * 1024:4 * 1024],
                              XT[0:64, 2 * 1024:3 * 1024],
                              pD[0:64, :], op=ALU.mult, sem=sV)
                kb.wait("sync", sV, ht_v)
                abo_dma = kb.dma("sync",
                                 R2[64 * (h % 2):64 * (h % 2) + 64,
                                    (h // 2) * T:(h // 2) * T + T],
                                 XT[0:64, 3 * 1024:4 * 1024], sAB)
                abo_marks.append(abo_dma)
                for i4 in range(4):
                    mxg[i4] = ht_v
            attnb_done = sV.n
            acc_free[0] = attnb_done
            acc_free[1] = attnb_done
            kb.wait("tensor", sAB, sAB.n)
            if DEBUG_STAGE == "abo":
                kb.wait("gpsimd", sAB, sAB.n)
                mk = kb.vop("engine_nop", sem=sV)
                dump_region(R2, sV, mk)
                return _finish(nc, kb)

            to_y = DEBUG_STAGE == 3
            proj(WobT, 512, D, lambda ic: ABO(ic),
                 mk_res_epi(scr2, yT if to_y else scr1, None if to_y else R1, BOB,
                            sY if to_y else sST),
                 rhs_ready=attnb_done)

        if DEBUG_STAGE == 0:
            kb.wait("vector", sS, sS.n)
            stats_chunks(R1)
            ln4 = ln_apply(R1, ln_finalize())
            ffn(False, R1, None, scr1, yT, sY, ln4)

        kb.wait("sync", sY, sY.n)
        return _finish(nc, kb)


_last_kb = None
def _finish(nc, kb):
    global _last_kb
    _last_kb = kb
    stuck = kb.check_feasible()
    if stuck:
        raise RuntimeError(f"static deadlock: {stuck[:5]}")
    with nc.Block() as block:
        @block.tensor
        def _(e):
            for f in kb.st["tensor"]:
                f(e)
        @block.vector
        def _(e):
            for f in kb.st["vector"]:
                f(e)
        @block.scalar
        def _(e):
            for f in kb.st["scalar"]:
                f(e)
        @block.gpsimd
        def _(e):
            for f in kb.st["gpsimd"]:
                f(e)
        @block.sync
        def _(e):
            for f in kb.st["sync"]:
                f(e)
    return nc


def _reshape_bias(v):
    v = np.ascontiguousarray(np.asarray(v, dtype=np.float32))
    n = v.shape[0]
    return np.ascontiguousarray(v.reshape(n // 128, 128).T)


def kernel(**inputs):
    x = np.asarray(inputs["L0_attn_a_hidden_states"], dtype=np.float32)
    f = lambda k: np.ascontiguousarray(np.asarray(inputs[k], dtype=np.float32))
    Wq_, Wk_, Wv_, Wo_ = f("L0_attn_a_Wq"), f("L0_attn_a_Wk"), f("L0_attn_a_Wv"), f("L0_attn_a_Wo")
    Wqk_w = f("L0_attn_b_Wqk_w")
    Wv_w = f("L0_attn_b_Wv_w")
    Wo_w = f("L0_attn_b_Wo_w")
    qweight = np.ascontiguousarray(np.asarray(inputs["L0_ffn_qweight"], dtype=np.int32))
    scales = f("L0_ffn_scales")
    zeros = f("L0_ffn_scaled_zeros")

    bq = _reshape_bias(inputs["L0_attn_a_bq"])
    bk = _reshape_bias(inputs["L0_attn_a_bk"])
    bv = _reshape_bias(inputs["L0_attn_a_bv"])
    bo = _reshape_bias(inputs["L0_attn_a_bo"])
    fb = _reshape_bias(inputs["L0_ffn_bias"])
    bob = _reshape_bias(inputs["L0_attn_b_Wo_b"])
    wqkb = np.asarray(inputs["L0_attn_b_Wqk_b"], dtype=np.float32)
    bqb = _reshape_bias(wqkb[0::2])
    bkb = _reshape_bias(wqkb[1::2])
    bvb = _reshape_bias(inputs["L0_attn_b_Wv_b"])
    bvecs = np.ascontiguousarray(
        np.concatenate([bq, bk, bv, bo, fb, bob, bqb, bkb, bvb], axis=1))

    shared = {
        "Wq": Wq_, "Wk": Wk_, "Wv": Wv_, "Wo": Wo_,
        "bvecs": bvecs,
        "gq_row": np.ascontiguousarray(f("L0_attn_a_gq").reshape(1, D)),
        "gk_row": np.ascontiguousarray(f("L0_attn_a_gk").reshape(1, D)),
        "qw2d": np.ascontiguousarray(qweight.reshape(D, D)),
        "scl_r": np.ascontiguousarray(scales[:, :, 0].reshape(NCH, 128, 32)),
        "zer_r": np.ascontiguousarray(zeros[:, :, 0].reshape(NCH, 128, 32)),
        "WqkqT": np.ascontiguousarray(Wqk_w[0::2].T),
        "WqkkT": np.ascontiguousarray(Wqk_w[1::2].T),
        "WvbT": np.ascontiguousarray(Wv_w.T),
        "WobT": np.ascontiguousarray(Wo_w.T),
        "idnf": np.eye(128, dtype=np.float32),
    }

    in_maps = []
    for c in range(8):
        b, q0 = c // 4, (c % 4) * T
        m = dict(shared)
        m["x0T"] = np.ascontiguousarray(x[b, q0:q0 + T, :].T)
        in_maps.append(m)

    nc = build_nc()
    res = run_bass_kernel_spmd(nc, in_maps, core_ids=list(range(8)))

    out = np.empty((2, 4096, D), dtype=np.float32)
    for c in range(8):
        b, q0 = c // 4, (c % 4) * T
        out[b, q0:q0 + T, :] = res.results[c]["yT"].T
    return out


# version 05
# speedup vs baseline: 1.1658x; 1.1658x over previous
import math
import contextlib
import numpy as np
import ml_dtypes
import concourse.bass as bass
import concourse.mybir as mybir
from concourse.bass_utils import run_bass_kernel_spmd

F32 = mybir.dt.float32
BF16 = mybir.dt.bfloat16
I32 = mybir.dt.int32
AF = mybir.ActivationFunctionType
ALU = mybir.AluOpType
AXX = mybir.AxisListType.X

D = 4096
T = 1024
NCH = 32
NH_A = 32
EPS_LN = 1e-5
EPS_RMS = 1e-6
CA_SHIFT = 20.0
ISQ_A = 1.0 / math.sqrt(128.0)
ISQ_B = 1.0 / math.sqrt(512.0)

DEBUG_STAGE = 0


class Sem:
    __slots__ = ("h", "n", "name")
    def __init__(self, h, name=""):
        self.h = h
        self.n = 0
        self.name = name


INTERP_SAFE = False


class KB:
    def __init__(self):
        self.st = {e: [] for e in ("tensor", "vector", "scalar", "gpsimd", "sync")}
        self.meta = {e: [] for e in ("tensor", "vector", "scalar", "gpsimd", "sync")}

    def emit(self, eng, fn, meta=None):
        self.st[eng].append(fn)
        self.meta[eng].append(meta or ("op",))

    def wait(self, eng, sem, val):
        if val > 0:
            self.emit(eng, lambda e, s=sem.h, v=val: e.wait_ge(s, v),
                      meta=("wait", sem.name, val))

    def check_feasible(self):
        pos = {e: 0 for e in self.meta}
        sems = {}
        progressed = True
        while progressed:
            progressed = False
            for e, ms in self.meta.items():
                while pos[e] < len(ms):
                    m = ms[pos[e]]
                    if m[0] == "wait" and sems.get(m[1], 0) < m[2]:
                        break
                    if m[0] == "inc":
                        sems[m[1]] = sems.get(m[1], 0) + m[2]
                    pos[e] += 1
                    progressed = True
        stuck = [(e, pos[e], self.meta[e][pos[e]])
                 for e in self.meta if pos[e] < len(self.meta[e])]
        return stuck or None

    def mm(self, out, lhsT, rhs, start, stop, sem=None):
        if sem is None:
            self.emit("tensor", lambda e, o=out, a=lhsT, b=rhs, s1=start, s2=stop:
                      e.matmul(o, a, b, start=s1, stop=s2))
            return None
        sem.n += 1
        self.emit("tensor", lambda e, o=out, a=lhsT, b=rhs, s1=start, s2=stop, s=sem.h:
                  e.matmul(o, a, b, start=s1, stop=s2).then_inc(s, 1),
                  meta=("inc", sem.name, 1))
        return sem.n

    def trans(self, out, in_, iden, sem=None):
        if sem is None:
            self.emit("tensor", lambda e, o=out, i=in_, d=iden: e.transpose(o, i, d))
            return None
        sem.n += 1
        self.emit("tensor", lambda e, o=out, i=in_, d=iden, s=sem.h:
                  e.transpose(o, i, d).then_inc(s, 1),
                  meta=("inc", sem.name, 1))
        return sem.n

    def dma(self, eng, out, in_, sem):
        if INTERP_SAFE and sem.n > 0:
            self.wait(eng, sem, sem.n)
        sem.n += 16
        self.emit(eng, lambda e, o=out, i=in_, s=sem.h:
                  e.dma_start(out=o, in_=i).then_inc(s, 16),
                  meta=("inc", sem.name, 16))
        return sem.n

    def act(self, out, in_, func, bias, scale, sem=None, accum=None):
        if sem is None:
            self.emit("scalar", lambda e, o=out, i=in_, f=func, b=bias, sc=scale, a=accum:
                      e.activation(o, i, f, bias=b, scale=sc, accum_out=a))
            return None
        sem.n += 1
        self.emit("scalar", lambda e, o=out, i=in_, f=func, b=bias, sc=scale, a=accum, s=sem.h:
                  e.activation(o, i, f, bias=b, scale=sc, accum_out=a).then_inc(s, 1),
                  meta=("inc", sem.name, 1))
        return sem.n

    def scopy(self, out, in_, sem=None, scale=1.0, bias=0.0):
        if sem is None:
            self.emit("scalar", lambda e, o=out, i=in_, b=bias, sc=scale:
                      e.activation(o, i, AF.Copy, bias=b, scale=sc))
            return None
        sem.n += 1
        self.emit("scalar", lambda e, o=out, i=in_, b=bias, sc=scale, s=sem.h:
                  e.activation(o, i, AF.Copy, bias=b, scale=sc).then_inc(s, 1),
                  meta=("inc", sem.name, 1))
        return sem.n

    def vop(self, name, *args, sem=None, **kw):
        if sem is None:
            self.emit("vector", lambda e, n=name, a=args, k=dict(kw): getattr(e, n)(*a, **k))
            return None
        sem.n += 1
        self.emit("vector", lambda e, n=name, a=args, k=dict(kw), s=sem.h:
                  getattr(e, n)(*a, **k).then_inc(s, 1),
                  meta=("inc", sem.name, 1))
        return sem.n

    def gop(self, name, *args, sem=None, **kw):
        if sem is None:
            self.emit("gpsimd", lambda e, n=name, a=args, k=dict(kw): getattr(e, n)(*a, **k))
            return None
        sem.n += 1
        self.emit("gpsimd", lambda e, n=name, a=args, k=dict(kw), s=sem.h:
                  getattr(e, n)(*a, **k).then_inc(s, 1),
                  meta=("inc", sem.name, 1))
        return sem.n

    def coll(self, kind, groups, ins, outs, sem):
        sem.n += 1
        self.emit("gpsimd", lambda e, kk=kind, g=groups, ii=ins, uu=outs, s=sem.h:
                  e.collective_compute(kk, ALU.bypass, replica_groups=g,
                                       ins=ii, outs=uu).then_inc(s),
                  meta=("inc", sem.name, 16))
        return sem.n


def build_nc():
    nc = bass.Bass("TRN2", target_bir_lowering=False)

    x0T = nc.dram_tensor("x0T", [D, T], F32, kind="ExternalInput").ap()
    Wq = nc.dram_tensor("Wq", [D, D], F32, kind="ExternalInput").ap()
    Wk = nc.dram_tensor("Wk", [D, D], F32, kind="ExternalInput").ap()
    Wv = nc.dram_tensor("Wv", [D, D], F32, kind="ExternalInput").ap()
    Wo = nc.dram_tensor("Wo", [D, D], F32, kind="ExternalInput").ap()
    bvecs = nc.dram_tensor("bvecs", [128, 6 * NCH + 12], F32, kind="ExternalInput").ap()
    gq_row = nc.dram_tensor("gq_row", [1, D], F32, kind="ExternalInput").ap()
    gk_row = nc.dram_tensor("gk_row", [1, D], F32, kind="ExternalInput").ap()
    qw2d = nc.dram_tensor("qw2d", [D, D], I32, kind="ExternalInput").ap()
    scl_r = nc.dram_tensor("scl_r", [NCH, 128, 32], F32, kind="ExternalInput").ap()
    zer_r = nc.dram_tensor("zer_r", [NCH, 128, 32], F32, kind="ExternalInput").ap()
    WqkqT = nc.dram_tensor("WqkqT", [D, 512], F32, kind="ExternalInput").ap()
    WqkkT = nc.dram_tensor("WqkkT", [D, 512], F32, kind="ExternalInput").ap()
    WvbT = nc.dram_tensor("WvbT", [D, 512], F32, kind="ExternalInput").ap()
    WobT = nc.dram_tensor("WobT", [512, D], F32, kind="ExternalInput").ap()
    idnf = nc.dram_tensor("idnf", [128, 128], F32, kind="ExternalInput").ap()

    yT = nc.dram_tensor("yT", [D, T], F32, kind="ExternalOutput").ap()

    scr1 = nc.dram_tensor("scr1", [D, T], F32).ap()
    scr2 = nc.dram_tensor("scr2", [D, T], F32).ap()
    wts = nc.dram_tensor("wts", [NCH * 128, D], BF16).ap()
    mxscr = nc.dram_tensor("mxscr", [8, 128], F32).ap()
    dnscr = nc.dram_tensor("dnscr", [1, 1024], F32).ap()

    ka_in = nc.dram_tensor("ka_in", [D, T], BF16)
    ka_out = nc.dram_tensor("ka_out", [4 * D, T], BF16)
    va_in = nc.dram_tensor("va_in", [NH_A * 8 * 128, 128], BF16)
    va_out = nc.dram_tensor("va_out", [4 * NH_A * 8 * 128, 128], BF16)
    kb_in = nc.dram_tensor("kb_in", [512, T], BF16)
    kb_out = nc.dram_tensor("kb_out", [4 * 512, T], BF16)
    vb_in = nc.dram_tensor("vb_in", [8 * T, 64], BF16)
    vb_out = nc.dram_tensor("vb_out", [4 * 8 * T, 64], BF16)
    groups = [[0, 1, 2, 3], [4, 5, 6, 7]]

    with contextlib.ExitStack() as stack:
        sbt = lambda name, shape, dt: stack.enter_context(nc.sbuf_tensor(name, shape, dt))
        R1 = sbt("R1", [128, NCH * T], BF16)
        R2 = sbt("R2", [128, NCH * T], BF16)
        WS = sbt("WS", [128, 8192], BF16)
        XT = sbt("XT", [128, 8192], BF16)
        WDF = sbt("WDF", [128, 1024], F32)
        STF = [sbt(f"STF{i}", [128, 512], F32) for i in range(2)]
        X0F = [sbt(f"X0F{i}", [128, 512], F32) for i in range(2)]
        BIAS = sbt("BIAS", [128, 6 * NCH + 12], F32)
        SCL = sbt("SCL", [128, 64], F32)
        F32R = sbt("F32R", [128, 3072], F32)
        RB = sbt("RB", [1, 5120], BF16)
        IDN = sbt("IDN", [128, 128], F32)
        ONEC = sbt("ONEC", [128, 1], BF16)
        ONERF = sbt("ONERF", [1, 128], BF16)
        ONEF32 = sbt("ONEF32", [1, 128], F32)
        NEGCA = sbt("NEGCA", [128, 1], F32)
        ZEROC = sbt("ZEROC", [128, 1], F32)
        EPSC = sbt("EPSC", [1, 2], F32)
        MXQ = sbt("MXQ", [128, 16], F32)
        MXT = sbt("MXT", [8, 128], F32)
        QTL = sbt("QTL", [65, 1024], BF16)

        Mrow = F32R[0:1, 0:1024]
        VArow = F32R[0:1, 1024:2048]
        TMProw = F32R[0:1, 2048:3072]
        RSTD = Mrow
        RCP = VArow
        DN0 = TMProw
        DN64 = F32R[64:65, 2048:3072]
        GAINS = RB[0:1, 0:4096]
        RBROW = RB[0:1, 4096:5120]

        pS0 = stack.enter_context(nc.psum_tensor("pS0", [128, 1024], F32))
        pS1 = stack.enter_context(nc.psum_tensor("pS1", [128, 1024], F32))
        pO = stack.enter_context(nc.psum_tensor("pO", [128, 1024], F32))
        pD = stack.enter_context(nc.psum_tensor("pD", [128, 1024], F32))

        mks = lambda name: Sem(stack.enter_context(nc.semaphore(name)), name)
        sW, sX, sST, sKV = mks("sW"), mks("sX"), mks("sST"), mks("sKV")
        sT, sV, sS, sG = mks("sT"), mks("sV"), mks("sS"), mks("sG")
        sC1, sC2, sC3, sC4 = mks("sC1"), mks("sC2"), mks("sC3"), mks("sC4")
        sY = mks("sY")
        sAB = mks("sAB")
        sKB = mks("sKB")
        sNR = mks("sNR")
        sNG = mks("sNG")
        sWA, sWB = mks("sWA"), mks("sWB")
        sR = [mks(f"sR{i}") for i in range(4)]

        kb = KB()

        def rch(region, ch):
            return region[:, ch * T:(ch + 1) * T]

        kb.gop("memset", ONEC[:, :], 1.0)
        kb.gop("memset", ONERF[:, :], 1.0)
        kb.gop("memset", ONEF32[:, :], 1.0)
        kb.gop("memset", NEGCA[:, :], -CA_SHIFT)
        kb.gop("memset", ZEROC[:, :], 0.0)
        kb.gop("memset", EPSC[0:1, 0:1], EPS_LN)
        kb.gop("memset", EPSC[0:1, 1:2], EPS_RMS)
        kb.gop("engine_nop", sem=sG)
        consts_g = sG.n
        kb.dma("sync", IDN[:, :], idnf[:, :], sX)
        kb.dma("sync", BIAS[:, :], bvecs[:, :], sX)
        consts_x = sX.n
        BQ = BIAS[:, 0 * NCH:1 * NCH]
        BK = BIAS[:, 1 * NCH:2 * NCH]
        BV = BIAS[:, 2 * NCH:3 * NCH]
        BO = BIAS[:, 3 * NCH:4 * NCH]
        FB = BIAS[:, 4 * NCH:5 * NCH]
        BOB = BIAS[:, 5 * NCH:6 * NCH]
        BQB = BIAS[:, 6 * NCH:6 * NCH + 4]
        BKB = BIAS[:, 6 * NCH + 4:6 * NCH + 8]
        BVB = BIAS[:, 6 * NCH + 8:6 * NCH + 12]

        xld = [kb.dma("gpsimd", rch(R1, ch), x0T[ch * 128:(ch + 1) * 128, :], sW)
               for ch in range(NCH)]
        kb.wait("vector", sG, consts_g)
        kb.wait("tensor", sG, consts_g)
        kb.wait("tensor", sX, consts_x)

        def stats_chunks(region, wait_ld=None):
            prev_mm = 0
            if wait_ld is not None:
                kb.wait("vector", sW, wait_ld[-1])
            for ch in range(NCH):
                kb.wait("vector", sT, prev_mm)
                vv = kb.vop("tensor_tensor", XT[:, 0:T], rch(region, ch),
                            rch(region, ch), op=ALU.mult, sem=sV)
                kb.wait("tensor", sV, vv)
                st, sp = ch == 0, ch == NCH - 1
                xc = rch(region, ch)
                kb.mm(pO[0:1, 0:512], ONEC[:, :], xc[:, 0:512], st, sp)
                kb.mm(pO[0:1, 512:1024], ONEC[:, :], xc[:, 512:1024], st, sp)
                kb.mm(pD[0:1, 0:512], ONEC[:, :], XT[:, 0:512], st, sp)
                prev_mm = kb.mm(pD[0:1, 512:1024], ONEC[:, :], XT[:, 512:1024], st, sp,
                                sem=sT)

        acc_free = {0: 0, 1: 0}
        stats_chunks(R1, wait_ld=xld)

        def ln_finalize():
            kb.wait("vector", sT, sT.n)
            va = kb.vop("tensor_scalar", Mrow, pO[0:1, :], 1.0 / D, None, op0=ALU.mult,
                        sem=sV)
            vb = kb.vop("tensor_scalar", VArow, pD[0:1, :], 1.0 / D, None, op0=ALU.mult,
                        sem=sV)
            kb.wait("vector", sV, vb)
            vc = kb.vop("tensor_tensor", TMProw, Mrow, Mrow, op=ALU.mult, sem=sV)
            kb.wait("vector", sV, vc)
            vv = kb.vop("tensor_tensor", VArow, VArow, TMProw, op=ALU.subtract, sem=sV)
            kb.wait("tensor", sV, sV.n)
            kb.mm(pS0[:, 0:512], ONEF32[0:1, :], Mrow[:, 0:512], True, True)
            tm = kb.mm(pS0[:, 512:1024], ONEF32[0:1, :], Mrow[:, 512:1024], True, True,
                       sem=sT)
            kb.wait("scalar", sV, vv)
            sl = kb.act(VArow, VArow, AF.Ln, EPSC[0:1, 0:1], 1.0, sem=sS)
            kb.wait("scalar", sS, sl)
            ss = kb.act(VArow, VArow, AF.Exp, ZEROC[0:1, 0:1], -0.5, sem=sS)
            kb.wait("scalar", sS, ss)
            kb.wait("scalar", sT, tm)
            sr = kb.scopy(RBROW, VArow, sem=sS)
            kb.wait("tensor", sS, sr)
            kb.mm(pS1[:, 0:512], ONERF[0:1, :], RBROW[:, 0:512], True, True)
            tt = kb.mm(pS1[:, 512:1024], ONERF[0:1, :], RBROW[:, 512:1024], True, True,
                       sem=sT)
            return tt

        def ln_apply(region, mr_ready):
            kb.wait("vector", sT, mr_ready)
            kb.wait("vector", sST, sST.n)
            kb.wait("vector", sY, sY.n)
            kb.wait("vector", sS, sS.n)
            for ch in range(NCH):
                xc = rch(region, ch)
                for tb in range(2):
                    hs = slice(tb * 512, (tb + 1) * 512)
                    ve = kb.vop("tensor_tensor", STF[tb][:, :], xc[:, hs],
                                pS0[:, hs], op=ALU.subtract, sem=sV)
                    kb.wait("vector", sV, ve)
                    vf = kb.vop("tensor_tensor", xc[:, hs], STF[tb][:, :],
                                pS1[:, hs], op=ALU.mult, sem=sV)
                    kb.wait("vector", sV, vf)
            m = kb.vop("engine_nop", sem=sV)
            acc_free[0] = m
            acc_free[1] = m
            return m

        def dump_region(region, marker_sem, marker):
            kb.wait("gpsimd", marker_sem, marker)
            for ch in range(NCH):
                kb.dma("gpsimd", yT[ch * 128:(ch + 1) * 128, :], rch(region, ch), sY)
            kb.wait("sync", sY, sY.n)

        if DEBUG_STAGE == "xb":
            mk = kb.vop("engine_nop", sem=sV)
            dump_region(R1, sV, mk)
            return _finish(nc, kb)
        ln1 = ln_apply(R1, ln_finalize())
        if DEBUG_STAGE == "ln1":
            dump_region(R1, sV, ln1)
            return _finish(nc, kb)

        def proj(W, n_in, n_out, rhs_ch, epi, rhs_ready=None):
            n_ic = n_in // 128
            n_oc = n_out // 128
            Wr = W.rearrange("(ic p) o -> p ic o", p=128)
            ldv, mmdone = {}, {}
            if rhs_ready:
                kb.wait("tensor", sV, rhs_ready)
            kb.wait("gpsimd", sT, sT.n)
            kb.wait("gpsimd", sST, sST.n)
            kb.wait("gpsimd", sAB, sAB.n)
            kb.wait("gpsimd", sNR, sNR.n)
            kb.wait("gpsimd", sNG, sNG.n)
            for oc in range(n_oc):
                slot = oc % 2
                wsem = sWA if slot == 0 else sWB
                if oc >= 2:
                    kb.wait("gpsimd", sT, mmdone[oc - 2])
                half = max(n_ic // 2, 1) if n_ic >= 8 else n_ic
                for g in range(n_ic // half):
                    kb.dma("gpsimd",
                           WS[:, slot * 4096 + g * half * 128:
                              slot * 4096 + (g + 1) * half * 128],
                           Wr[:, g * half:(g + 1) * half, oc * 128:(oc + 1) * 128], wsem)
                ldv[oc] = wsem.n
                kb.wait("tensor", wsem, ldv[oc])
                acc = pS0 if oc % 2 == 0 else pS1
                kb.wait("tensor", sV, acc_free[oc % 2])
                for tb in range(2):
                    for ic in range(n_ic):
                        kb.mm(acc[:, tb * 512:(tb + 1) * 512],
                              WS[:, slot * 4096 + ic * 128:slot * 4096 + (ic + 1) * 128],
                              rhs_ch(ic)[:, tb * 512:(tb + 1) * 512],
                              ic == 0, ic == n_ic - 1,
                              sem=(sT if (ic == n_ic - 1 and tb == 1) else None))
                mmdone[oc] = sT.n
                acc_free[oc % 2] = epi(oc, acc, sT.n)

        def mk_epi_raw(dstreg, bias_cols, stats):
            state = {"mm": 0}
            def epi(oc, acc, tA):
                kb.wait("vector", sT, tA)
                dst = rch(dstreg, oc)
                vv = kb.vop("tensor_scalar", dst, acc[:, :], bias_cols[:, oc:oc + 1],
                            None, op0=ALU.add, sem=sV)
                if not stats:
                    return vv
                kb.wait("vector", sT, state["mm"])
                kb.wait("vector", sV, vv)
                v2 = kb.vop("tensor_tensor", XT[:, 0:T], dst, dst, op=ALU.mult, sem=sV)
                kb.wait("tensor", sV, v2)
                st, sp = oc == 0, oc == NCH - 1
                kb.mm(pD[0:1, 0:512], ONEC[:, :], XT[:, 0:512], st, sp)
                state["mm"] = kb.mm(pD[0:1, 512:1024], ONEC[:, :], XT[:, 512:1024],
                                    st, sp, sem=sT)
                return v2
            return epi

        proj(Wk, D, D, lambda ic: rch(R1, ic), mk_epi_raw(R2, BK, True), rhs_ready=ln1)
        if DEBUG_STAGE == "kraw0":
            mk = kb.vop("engine_nop", sem=sV)
            dump_region(R2, sV, mk)
            return _finish(nc, kb)

        def rms_apply(grow_dram, out_cb):
            kb.wait("gpsimd", sT, sT.n)
            gld = kb.dma("gpsimd", GAINS, grow_dram[0:1, :], sNG)
            kb.wait("vector", sT, sT.n)
            vv = kb.vop("tensor_scalar", RSTD, pD[0:1, :], 1.0 / D, None,
                        op0=ALU.mult, sem=sV)
            kb.wait("scalar", sV, vv)
            s1_ = kb.act(RSTD, RSTD, AF.Ln, EPSC[0:1, 1:2], 1.0, sem=sS)
            kb.wait("scalar", sS, s1_)
            s2_ = kb.act(RSTD, RSTD, AF.Exp, ZEROC[0:1, 0:1], -0.5, sem=sS)
            kb.wait("scalar", sS, s2_)
            ss = kb.scopy(RBROW, RSTD, sem=sS)
            kb.wait("tensor", sS, ss)
            kb.wait("tensor", sNG, gld)
            kb.wait("tensor", sS, sS.n)
            gfree = {0: 0, 1: acc_free[0]}
            for oc in range(NCH):
                gb = oc % 2
                G = pO if gb == 0 else pS0
                kb.wait("tensor", sV, gfree[gb])
                kb.mm(G[:, 0:512], GAINS[0:1, oc * 128:(oc + 1) * 128], RBROW[:, 0:512],
                      True, True)
                tg = kb.mm(G[:, 512:1024], GAINS[0:1, oc * 128:(oc + 1) * 128],
                           RBROW[:, 512:1024], True, True, sem=sT)
                kb.wait("vector", sT, tg)
                gfree[gb] = out_cb(oc, G)
            acc_free[0] = max(acc_free[0], gfree[1])

        KH = [XT[:, 0:1024], XT[:, 1024:2048]]
        kexp = {}
        def k_out(oc, G):
            slot = oc % 2
            if oc >= 2:
                kb.wait("vector", sST, sST.n)
            vv = kb.vop("tensor_tensor", KH[slot], rch(R2, oc), G[:, :],
                        op=ALU.mult, sem=sV)
            kb.wait("sync", sV, vv)
            kexp[oc] = kb.dma("sync", ka_in.ap()[oc * 128:(oc + 1) * 128, :],
                              KH[slot], sST)
            return vv
        rms_apply(gk_row, k_out)
        kb.wait("gpsimd", sST, sST.n)
        kb.coll("AllGather", groups, [ka_in.ap().opt()], [ka_out.ap().opt()], sC1)
        kb.wait("gpsimd", sC1, 1)

        def v_epi_mk(dst_block):
            def epi(oc, acc, tA):
                kb.wait("vector", sT, tA)
                for tb in range(2):
                    vv = kb.vop("tensor_scalar", STF[tb][:, :],
                                acc[:, tb * 512:(tb + 1) * 512], BV[:, oc:oc + 1],
                                None, op0=ALU.add, sem=sV)
                    kb.wait("tensor", sV, vv)
                    kb.wait("tensor", sS, sS.n)
                    for j in range(4):
                        tc = tb * 4 + j
                        kb.trans(pO[:, tc * 128:(tc + 1) * 128], STF[tb][:, j * 128:(j + 1) * 128],
                                 IDN[:, :], sem=(sT if j == 3 else None))
                    kb.wait("scalar", sT, sT.n)
                    for j in range(4):
                        tc = tb * 4 + j
                        kb.scopy(dst_block(tc)[:, oc * 128:(oc + 1) * 128],
                                 pO[:, tc * 128:(tc + 1) * 128],
                                 sem=(sS if j == 3 else None))
                return sV.n
            return epi
        proj(Wv, D, D, lambda ic: rch(R1, ic),
             v_epi_mk(lambda tc: R2[:, tc * 4096:(tc + 1) * 4096]))
        va_view = va_in.ap().rearrange("(h tc t) d -> tc t h d", h=NH_A, tc=8)
        kb.wait("sync", sS, sS.n)
        for tc in range(8):
            kb.dma("sync", va_view[tc],
                   R2[:, tc * 4096:(tc + 1) * 4096].rearrange("p (h d) -> p h d", d=128),
                   sST)
        kb.wait("gpsimd", sST, sST.n)
        kb.coll("AllGather", groups, [va_in.ap().opt()], [va_out.ap().opt()], sC2)
        kb.wait("gpsimd", sC2, 1)

        if DEBUG_STAGE == "qbarrier":
            kb.wait("gpsimd", sC1, 1)
            kb.wait("gpsimd", sC2, 1)
            kb.wait("gpsimd", sST, sST.n)
        if DEBUG_STAGE == "qcoll":
            kb.wait("gpsimd", sC1, 1)
            kb.wait("gpsimd", sC2, 1)
        if DEBUG_STAGE == "qexp":
            kb.wait("gpsimd", sST, sST.n)
        proj(Wq, D, D, lambda ic: rch(R1, ic), mk_epi_raw(R2, BQ, True))
        if DEBUG_STAGE == "xhat_after":
            mk = kb.vop("engine_nop", sem=sV)
            dump_region(R1, sV, mk)
            return _finish(nc, kb)
        def q_out(oc, G):
            return kb.vop("tensor_tensor", rch(R1, oc), rch(R2, oc), G[:, :],
                          op=ALU.mult, sem=sV)
        rms_apply(gq_row, q_out)
        qhat_done = sV.n
        if DEBUG_STAGE in ("qbarrier", "qcoll", "qexp"):
            dump_region(R2, sV, qhat_done)
            return _finish(nc, kb)
        if DEBUG_STAGE == "qhat":
            dump_region(R1, sV, qhat_done)
            return _finish(nc, kb)
        if DEBUG_STAGE == "kraw":
            dump_region(R2, sV, qhat_done)
            return _finish(nc, kb)

        ka_out_ap = ka_out.ap()
        va_view_o = va_out.ap().rearrange("(r h tc t) d -> r h t tc d",
                                          r=4, h=NH_A, tc=8)
        kb.wait("sync", sC1, 1)
        kb.wait("sync", sC2, 1)
        kb.wait("sync", sT, sT.n)
        ldmark = {}
        uses = {}
        def load_rank(i):
            h, r = divmod(i, 4)
            s = i % 4
            kb.dma("sync", WS[:, s * 2048:s * 2048 + 1024],
                   ka_out_ap[r * D + h * 128:r * D + (h + 1) * 128, :], sR[s])
            ldmark[i] = kb.dma(
                "sync",
                WS[:, s * 2048 + 1024:s * 2048 + 2048].rearrange("p (tc d) -> p tc d", d=128),
                va_view_o[r, h], sR[s])
        load_rank(0)
        load_rank(1)
        kb.wait("tensor", sV, qhat_done)
        head_tail_v = qhat_done
        expc, pvc, tsd = {}, {}, {}
        pv_next = [0]

        def emit_pv_upto(Fmax):
            while pv_next[0] <= Fmax:
                emit_pv(pv_next[0])
                pv_next[0] += 1

        def emit_pv(F):
            hh, cc = divmod(F, NCH)
            rr, c8 = divmod(cc, 8)
            s = (hh * 4 + rr) % 4
            vbuf = WS[:, s * 2048 + 1024:s * 2048 + 2048]
            kb.wait("tensor", sS, expc[F])
            st, sp = cc == 0, cc == NCH - 1
            if st and hh > 0:
                kb.wait("tensor", sV, head_tail_v)
            pbuf = XT[:, (F % 4) * 1024:(F % 4 + 1) * 1024]
            kb.mm(pO[:, 0:512], vbuf[:, c8 * 128:(c8 + 1) * 128], pbuf[:, 0:512],
                  st, sp)
            kb.mm(pO[:, 512:1024], vbuf[:, c8 * 128:(c8 + 1) * 128],
                  pbuf[:, 512:1024], st, sp)
            kb.mm(pD[0:1, 0:512], ONEC[:, :], pbuf[:, 0:512], st, sp)
            pvc[F] = kb.mm(pD[0:1, 512:1024], ONEC[:, :], pbuf[:, 512:1024],
                           st, sp, sem=sT)
            if c8 == 7:
                uses[hh * 4 + rr] = pvc[F]

        for F in range(NH_A * NCH):
            h, c = divmod(F, NCH)
            r, c8 = divmod(c, 8)
            i = h * 4 + r
            s = i % 4
            Sp = pS0 if (c + h) % 2 == 0 else pS1
            if c8 == 0:
                if i + 2 < NH_A * 4:
                    if i >= 2:
                        kb.wait("sync", sT, uses[i - 2])
                    load_rank(i + 2)
                kb.wait("tensor", sR[s], ldmark[i])
            kbuf = WS[:, s * 2048:s * 2048 + 1024]
            if c >= 2:
                kb.wait("tensor", sS, expc[F - 2])
            elif h > 0:
                if c == 0:
                    kb.wait("tensor", sS, expc[F - 1])
                else:
                    kb.wait("tensor", sV, head_tail_v)
            kb.mm(Sp[:, 0:512], kbuf[:, c8 * 128:(c8 + 1) * 128],
                  rch(R1, h)[:, 0:512], True, True)
            tsd[F] = kb.mm(Sp[:, 512:1024], kbuf[:, c8 * 128:(c8 + 1) * 128],
                           rch(R1, h)[:, 512:1024], True, True, sem=sT)
            kb.wait("scalar", sT, tsd[F])
            if F >= 4:
                kb.wait("scalar", sT, pvc[F - 4])
            pbuf = XT[:, (F % 4) * 1024:(F % 4 + 1) * 1024]
            expc[F] = kb.act(pbuf, Sp[:, :], AF.Exp, NEGCA[:, :], ISQ_A, sem=sS)
            emit_pv_upto(F - 1)
            if c == NCH - 1:
                emit_pv_upto(F)
                kb.wait("vector", sT, pvc[F])
                vr = kb.vop("reciprocal", RCP, pD[0:1, :], sem=sV)
                kb.wait("vector", sV, vr)
                rb = kb.vop("tensor_copy", RBROW, RCP, sem=sV)
                kb.wait("scalar", sT, pvc[F])
                kb.wait("scalar", sV, head_tail_v)
                oc_cp = kb.scopy(XT[:, 4096:5120], pO[:, :], sem=sS)
                kb.wait("tensor", sV, rb)
                kb.wait("tensor", sS, expc[F - 1])
                SpE = pS0 if (30 + h) % 2 == 0 else pS1
                kb.mm(SpE[:, 0:512], ONERF[0:1, :], RBROW[:, 0:512], True, True)
                tb1 = kb.mm(SpE[:, 512:1024], ONERF[0:1, :], RBROW[:, 512:1024],
                            True, True, sem=sT)
                kb.wait("vector", sT, tb1)
                kb.wait("vector", sS, oc_cp)
                head_tail_v = kb.vop("tensor_tensor", rch(R2, h), XT[:, 4096:5120],
                                     SpE[:, :], op=ALU.mult, sem=sV)
        attn_done = sV.n
        acc_free[0] = head_tail_v
        acc_free[1] = head_tail_v
        if DEBUG_STAGE == "attn":
            dump_region(R2, sV, attn_done)
            return _finish(nc, kb)

        resid_ld = {}
        stf_free = [0, 0]
        x0f_free = [0, 0]
        stf_scpy = [0, 0]

        def mk_res_epi(src_ap, dst_ap, dst_reg, bias_cols, sto):
            def epi(oc, acc, tA):
                if oc == 0:
                    resid_ld.clear()
                    stf_free[0] = stf_free[1] = 0
                    kb.wait("vector", sST, sST.n)
                    kb.wait("vector", sY, sY.n)
                    kb.wait("vector", sS, sS.n)
                    stf_scpy[0] = stf_scpy[1] = 0
                    kb.wait("sync", sV, max(x0f_free))
                    kb.wait("sync", sST, sST.n)
                kb.wait("vector", sT, tA)
                for tb in range(2):
                    kb.wait("sync", sV, x0f_free[tb])
                    resid_ld[(oc, tb)] = kb.dma(
                        "sync", X0F[tb][:, :],
                        src_ap[oc * 128:(oc + 1) * 128, tb * 512:(tb + 1) * 512], sX)
                for tb in range(2):
                    kb.wait("vector", sX, resid_ld[(oc, 1)])
                    kb.wait("vector", sto, sto.n)
                    kb.wait("vector", sS, sS.n)
                    vv = kb.vop("scalar_tensor_tensor", STF[tb][:, :],
                                acc[:, tb * 512:(tb + 1) * 512],
                                bias_cols[:, oc:oc + 1], X0F[tb][:, :],
                                op0=ALU.add, op1=ALU.add, sem=sV)
                    x0f_free[tb] = vv
                    kb.wait("sync", sV, vv)
                    stf_free[tb] = kb.dma(
                        "sync", dst_ap[oc * 128:(oc + 1) * 128, tb * 512:(tb + 1) * 512],
                        STF[tb][:, :], sto)
                    if dst_reg is not None:
                        kb.wait("scalar", sV, vv)
                        stf_scpy[tb] = kb.scopy(
                            rch(dst_reg, oc)[:, tb * 512:(tb + 1) * 512],
                            STF[tb][:, :], sem=sS)
                return x0f_free[1]
            return epi

        to_y = DEBUG_STAGE == 1
        proj(Wo, D, D, lambda ic: rch(R2, ic),
             mk_res_epi(x0T, yT if to_y else scr1, None if to_y else R1, BO,
                        sY if to_y else sST),
             rhs_ready=attn_done)

        def ffn(first, rhs_reg, out_reg, src_ap, dst_ap, sto, rhs_ready):
            qwr = qw2d.rearrange("(oc p) i -> oc p i", p=128)
            epi = mk_res_epi(src_ap, dst_ap, out_reg, FB, sto)
            deq_used, mmdone, wt_st, wtld = {}, {}, {}, {}
            trdone, cpd = {}, {}
            if rhs_ready:
                kb.wait("tensor", sV, rhs_ready)
            kb.wait("gpsimd", sT, sT.n)
            kb.wait("gpsimd", sST, sST.n)
            kb.wait("scalar", sT, sT.n)
            kb.wait("scalar", sST, sST.n)
            for oc in range(NCH):
                if first:
                    if oc >= 1:
                        kb.wait("gpsimd", sV, deq_used[oc - 1])
                    kb.dma("gpsimd", WS[:, 0:2048], qwr[oc, :, 0:2048], sW)
                    kb.dma("gpsimd", WS[:, 2048:4096], qwr[oc, :, 2048:4096], sW)
                    kb.dma("gpsimd", SCL[:, 0:32], scl_r[oc], sW)
                    qldv = kb.dma("gpsimd", SCL[:, 32:64], zer_r[oc], sW)
                    kb.wait("vector", sW, qldv)
                    for w in range(8):
                        wh = (w % 2) * 512
                        g_t = trdone.get((oc, w - 2)) if w >= 2 else trdone.get((oc - 1, 6 + w))
                        if g_t:
                            kb.wait("vector", sT, g_t)
                        for j in range(4):
                            g = w * 4 + j
                            kb.vop("tensor_scalar", WDF[:, wh + j * 128:wh + (j + 1) * 128],
                                   WS[:, g * 128:(g + 1) * 128],
                                   SCL[:, g:g + 1], SCL[:, 32 + g:32 + g + 1],
                                   op0=ALU.mult, op1=ALU.add,
                                   sem=(sV if j == 3 else None))
                        dq = sV.n
                        kb.wait("tensor", sV, dq)
                        po_h = (w % 2) * 512
                        g_c = cpd.get((oc, w - 2)) if w >= 2 else cpd.get((oc - 1, 6 + w))
                        if g_c:
                            kb.wait("tensor", sS, g_c)
                        if oc >= 1 and w == 0:
                            kb.wait("scalar", sST, wt_st.get(oc - 1, 0))
                        for j in range(4):
                            kb.trans(pO[:, po_h + j * 128:po_h + (j + 1) * 128],
                                     WDF[:, wh + j * 128:wh + (j + 1) * 128],
                                     IDN[:, :], sem=(sT if j == 3 else None))
                        trdone[(oc, w)] = sT.n
                        kb.wait("scalar", sT, trdone[(oc, w)])
                        cpd[(oc, w)] = kb.scopy(WS[:, 4096 + w * 512:4096 + (w + 1) * 512],
                                                pO[:, po_h:po_h + 512], sem=sS)
                    deq_used[oc] = sV.n
                    wcopy = sS.n
                    wt = WS[:, 4096:8192]
                    kb.wait("sync", sS, wcopy)
                    wt_st[oc] = kb.dma("sync", wts[oc * 128:(oc + 1) * 128, :], wt, sST)
                    kb.wait("tensor", sS, wcopy)
                else:
                    slot = oc % 2
                    wsem = sKV if slot == 0 else sKB
                    if oc == 0:
                        kb.wait("scalar", sST, sST.n)
                    if oc >= 2:
                        kb.wait("scalar", sT, mmdone[oc - 2])
                    kb.dma("scalar", WS[:, slot * 4096:slot * 4096 + 4096],
                           wts[oc * 128:(oc + 1) * 128, :], wsem)
                    wtld[oc] = wsem.n
                    kb.wait("tensor", wsem, wtld[oc])
                    wt = WS[:, slot * 4096:slot * 4096 + 4096]
                acc = pS0 if oc % 2 == 0 else pS1
                kb.wait("tensor", sV, acc_free[oc % 2])
                for tb in range(2):
                    for ic in range(NCH):
                        kb.mm(acc[:, tb * 512:(tb + 1) * 512],
                              wt[:, ic * 128:(ic + 1) * 128],
                              rch(rhs_reg, ic)[:, tb * 512:(tb + 1) * 512],
                              ic == 0, ic == NCH - 1,
                              sem=(sT if (ic == NCH - 1 and tb == 1) else None))
                mmdone[oc] = sT.n
                acc_free[oc % 2] = epi(oc, acc, sT.n)

        if DEBUG_STAGE != 1:
            kb.wait("vector", sS, sS.n)
            if DEBUG_STAGE == "x1b":
                mk = kb.vop("engine_nop", sem=sV)
                kb.wait("gpsimd", sS, sS.n)
                dump_region(R1, sV, mk)
                return _finish(nc, kb)
            stats_chunks(R1)
            ln2 = ln_apply(R1, ln_finalize())
            if DEBUG_STAGE == "ln2":
                dump_region(R1, sV, ln2)
                return _finish(nc, kb)
            to_y = DEBUG_STAGE == 2
            ffn(True, R1, None if to_y else R2, scr1, yT if to_y else scr2,
                sY if to_y else sST, ln2)

        if DEBUG_STAGE in (0, 3, "abo"):
            kb.wait("vector", sS, sS.n)
            stats_chunks(R2)
            ln3 = ln_apply(R2, ln_finalize())

            QB = lambda ch: XT[:, 4096 + ch * T:4096 + (ch + 1) * T]
            ABO = lambda ch: rch(R2, ch)
            VBT = lambda tc: R1[:, tc * 512:(tc + 1) * 512]

            def qb_epi(oc, acc, tA):
                kb.wait("vector", sT, tA)
                return kb.vop("tensor_scalar", QB(oc), acc[:, :], BQB[:, oc:oc + 1],
                              None, op0=ALU.add, sem=sV)
            proj(WqkqT, D, 512, lambda ic: rch(R2, ic), qb_epi, rhs_ready=ln3)

            kexp_b = {}
            def kbp_epi(oc, acc, tA):
                slot = oc % 2
                kb.wait("vector", sT, tA)
                if oc >= 2:
                    kb.wait("vector", sST, sST.n)
                vv = kb.vop("tensor_scalar", KH[slot], acc[:, :],
                            BKB[:, oc:oc + 1], None, op0=ALU.add, sem=sV)
                kb.wait("sync", sV, vv)
                kexp_b[oc] = kb.dma("sync", kb_in.ap()[oc * 128:(oc + 1) * 128, :],
                                    KH[slot], sST)
                return vv
            proj(WqkkT, D, 512, lambda ic: rch(R2, ic), kbp_epi)
            kb.wait("gpsimd", sST, sST.n)
            kb.coll("AllGather", groups, [kb_in.ap().opt()], [kb_out.ap().opt()], sC3)
            kb.wait("gpsimd", sC3, 1)

            def vbp_epi(oc, acc, tA):
                kb.wait("vector", sT, tA)
                if oc < 2:
                    kb.wait("vector", sST, sST.n)
                    kb.wait("vector", sS, sS.n)
                for tb in range(2):
                    vv = kb.vop("tensor_scalar", STF[tb][:, :],
                                acc[:, tb * 512:(tb + 1) * 512], BVB[:, oc:oc + 1],
                                None, op0=ALU.add, sem=sV)
                    kb.wait("tensor", sV, vv)
                    kb.wait("tensor", sS, sS.n)
                    for j in range(4):
                        tc = tb * 4 + j
                        kb.trans(pO[:, tc * 128:(tc + 1) * 128],
                                 STF[tb][:, j * 128:(j + 1) * 128],
                                 IDN[:, :], sem=(sT if j == 3 else None))
                    kb.wait("scalar", sT, sT.n)
                    for j in range(4):
                        tc = tb * 4 + j
                        kb.scopy(VBT(tc)[:, oc * 128:(oc + 1) * 128],
                                 pO[:, tc * 128:(tc + 1) * 128],
                                 sem=(sS if j == 3 else None))
                return sV.n
            proj(WvbT, D, 512, lambda ic: rch(R2, ic), vbp_epi)
            vb_view = vb_in.ap().rearrange("(h tc t) d -> tc t h d", h=8, tc=8)
            kb.wait("sync", sS, sS.n)
            for tc in range(8):
                kb.dma("sync", vb_view[tc],
                       VBT(tc).rearrange("p (h d) -> p h d", d=64), sST)
            kb.wait("gpsimd", sST, sST.n)
            kb.coll("AllGather", groups, [vb_in.ap().opt()], [vb_out.ap().opt()], sC4)
            kb.wait("gpsimd", sC4, 1)

            kb.wait("scalar", sC3, 1)
            kb.wait("scalar", sC4, 1)
            kb.wait("scalar", sT, sT.n)
            kb.wait("gpsimd", sT, sT.n)
            kbo_ap = kb_out.ap()
            vbo_view = vb_out.ap().rearrange("(r h tc t) d -> r h t tc d",
                                             r=4, h=8, tc=8)
            kb.gop("memset", WS[64:65, 0:4096], 1.0)
            for s in range(2):
                base = 4096 + s * 520
                kb.gop("memset", WS[:, base + 64:base + 520:65], 1.0)
            kb.gop("engine_nop", sem=sG)
            kb.wait("scalar", sG, sG.n)

            ktb = WS[:, 0:4096]
            kldb = {}
            vldb = {}
            vsem = {}
            vuse = {}
            def load_kt(h, guard):
                kb.wait("scalar", sT, guard)
                for r in range(4):
                    kb.dma("scalar", WS[0:64, r * 1024:(r + 1) * 1024],
                           kbo_ap[r * 512 + h * 64:r * 512 + (h + 1) * 64, :], sKB)
                kldb[h] = sKB.n
            def load_vt(h, r, guard):
                s = (h * 4 + r) % 2
                kb.wait("scalar", sT, guard)
                base = 4096 + s * 520
                dst = WS[:, base:base + 520].rearrange("p (tc d) -> p tc d", d=65)[:, :, 0:64]
                vldb[(h, r)] = kb.dma("scalar", dst, vbo_view[r, h], sR[s])
                vsem[(h, r)] = s

            load_kt(0, 0)
            abo_marks = []
            qb_ready = sV.n
            mx_hist = {"store": 0, "qtl": 0}
            mxg = {0: acc_free[0], 1: acc_free[0], 2: acc_free[1], 3: acc_free[1]}
            ht_v = sV.n
            ht_s = 0
            prev_ts_b = 0
            last_vuse = {0: 0, 1: 0}
            for h in range(8):
                kb.wait("tensor", sKB, kldb[h])
                load_vt(h, 0, max(prev_ts_b, last_vuse[(h * 4) % 2]))
                load_vt(h, 1, max(prev_ts_b, last_vuse[(h * 4 + 1) % 2]))
                qrows = QB(h // 2)[64 * (h % 2):64 * (h % 2) + 64, :]
                kb.wait("sync", sT, prev_ts_b)
                kb.wait("sync", sV, qb_ready)
                qcp = kb.dma("sync", QTL[0:64, :], qrows, sNR)
                REG = [pS0[:, 0:512], pS0[:, 512:1024], pS1[:, 0:512], pS1[:, 512:1024]]
                kb.wait("tensor", sNR, qcp)
                for r in range(4):
                    for qt in range(8):
                        for k2 in range(2):
                            i4 = (r * 16 + qt * 2 + k2) % 4
                            kb.wait("tensor", sV, mxg[i4])
                            tsx = kb.mm(REG[i4], QTL[0:64, qt * 128:(qt + 1) * 128],
                                        ktb[0:64, r * 1024 + k2 * 512:r * 1024 + (k2 + 1) * 512],
                                        True, True, sem=sT)
                            kb.wait("vector", sT, tsx)
                            col = 8 + k2
                            kb.vop("reduce_max", MXQ[:, col:col + 1], REG[i4], axis=AXX,
                                   sem=sV)
                            mxg[i4] = sV.n
                            kb.wait("vector", sV, mxg[i4])
                        if r == 0:
                            vm = kb.vop("tensor_tensor", MXQ[:, qt:qt + 1], MXQ[:, 8:9],
                                        MXQ[:, 9:10], op=ALU.max, sem=sV)
                        else:
                            vm = kb.vop("tensor_tensor", MXQ[:, qt:qt + 1],
                                        MXQ[:, qt:qt + 1], MXQ[:, 8:9], op=ALU.max,
                                        sem=sV)
                            kb.wait("vector", sV, vm)
                            vm = kb.vop("tensor_tensor", MXQ[:, qt:qt + 1],
                                        MXQ[:, qt:qt + 1], MXQ[:, 9:10], op=ALU.max,
                                        sem=sV)
                        kb.wait("vector", sV, vm)
                vn = kb.vop("tensor_scalar", MXQ[:, 0:8], MXQ[:, 0:8], -1.0, None,
                            op0=ALU.mult, sem=sV)
                kb.wait("vector", sV, vn)
                mx_last = kb.vop("engine_nop", sem=sV)
                kb.wait("tensor", sV, mx_last)
                kb.wait("tensor", sV, ht_v)
                tmx = kb.trans(pD[0:8, 0:128], MXQ[:, 0:8], IDN[:, :], sem=sT)
                kb.wait("scalar", sT, tmx)
                kb.wait("scalar", sNR, mx_hist["store"])
                smx = kb.scopy(MXT[:, :], pD[0:8, 0:128], sem=sS)
                kb.wait("sync", sS, smx)
                mx_hist["store"] = kb.dma("sync", mxscr[:, :], MXT[:, :], sNR)
                kb.wait("gpsimd", sNR, mx_hist["store"])
                kb.wait("gpsimd", sT, prev_ts_b)
                qtl_s = kb.dma("gpsimd", QTL[64:65, :],
                               mxscr.rearrange("a b -> (a b)")[None, :], sNG)
                kb.wait("tensor", sNG, qtl_s)
                kb.wait("tensor", sV, mx_last)
                expc, pvc = {}, {}
                for c in range(32):
                    r, c8 = c // 8, c % 8
                    if c8 == 0 and 1 <= r < 3:
                        load_vt(h, r + 1, vuse[(h, r - 1)])
                    s = (h * 4 + r) % 2
                    vtb = WS[:, 4096 + s * 520:4096 + (s + 1) * 520]
                    if c8 == 0:
                        kb.wait("tensor", sR[vsem[(h, r)]], vldb[(h, r)])
                    Sp = pS0 if c % 2 == 0 else pS1
                    if c >= 2:
                        kb.wait("tensor", sS, expc[c - 2])
                    kb.mm(Sp[:, 0:512], ktb[0:65, c * 128:(c + 1) * 128],
                          QTL[0:65, 0:512], True, True)
                    ts = kb.mm(Sp[:, 512:1024], ktb[0:65, c * 128:(c + 1) * 128],
                               QTL[0:65, 512:1024], True, True, sem=sT)
                    kb.wait("scalar", sT, ts)
                    if c >= 3:
                        kb.wait("scalar", sT, pvc[c - 3])
                    pbuf = XT[:, (c % 3) * 1024:(c % 3 + 1) * 1024]
                    if c == 2:
                        kb.wait("scalar", sV, ht_v)
                    expc[c] = kb.act(pbuf, Sp[:, :], AF.Exp, ZEROC[:, :], 1.0, sem=sS)
                    kb.wait("tensor", sS, expc[c])
                    st, sp = c == 0, c == 31
                    if st:
                        kb.wait("tensor", sS, ht_s)
                    kb.mm(pO[0:65, 0:512], vtb[:, c8 * 65:(c8 + 1) * 65],
                          pbuf[:, 0:512], st, sp)
                    pvc[c] = kb.mm(pO[0:65, 512:1024], vtb[:, c8 * 65:(c8 + 1) * 65],
                                   pbuf[:, 512:1024], st, sp, sem=sT)
                    if c8 == 7:
                        vuse[(h, r)] = pvc[c]
                        last_vuse[(h * 4 + r) % 2] = pvc[c]
                prev_ts_b = pvc[31]
                if h + 1 < 8:
                    load_kt(h + 1, prev_ts_b)
                kb.wait("scalar", sT, pvc[31])
                dcp = kb.scopy(DN64, pO[64:65, :], sem=sS)
                kb.wait("sync", sS, dcp)
                dst1 = kb.dma("sync", dnscr[0:1, :], DN64, sNR)
                kb.wait("sync", sNR, dst1)
                dld = kb.dma("sync", DN0, dnscr[0:1, :], sNR)
                kb.wait("vector", sNR, dld)
                vr = kb.vop("reciprocal", RCP, DN0, sem=sV)
                kb.wait("scalar", sV, vr)
                kb.wait("scalar", sT, sT.n)
                srb = kb.scopy(RBROW, RCP, scale=ISQ_B, sem=sS)
                kb.wait("tensor", sS, srb)
                kb.mm(pD[0:64, 0:512], ONERF[0:1, 0:64], RBROW[:, 0:512], True, True)
                tbb = kb.mm(pD[0:64, 512:1024], ONERF[0:1, 0:64], RBROW[:, 512:1024],
                            True, True, sem=sT)
                kb.wait("scalar", sT, tbb)
                kb.wait("scalar", sV, ht_v)
                ht_s = kb.scopy(XT[0:64, 2 * 1024:3 * 1024], pO[0:64, :], sem=sS)
                kb.wait("vector", sS, ht_s)
                kb.wait("vector", sAB, abo_marks[-1] if abo_marks else 0)
                ht_v = kb.vop("tensor_tensor", XT[0:64, 3 * 1024:4 * 1024],
                              XT[0:64, 2 * 1024:3 * 1024],
                              pD[0:64, :], op=ALU.mult, sem=sV)
                kb.wait("sync", sV, ht_v)
                abo_dma = kb.dma("sync",
                                 R2[64 * (h % 2):64 * (h % 2) + 64,
                                    (h // 2) * T:(h // 2) * T + T],
                                 XT[0:64, 3 * 1024:4 * 1024], sAB)
                abo_marks.append(abo_dma)
                for i4 in range(4):
                    mxg[i4] = ht_v
            attnb_done = sV.n
            acc_free[0] = attnb_done
            acc_free[1] = attnb_done
            kb.wait("tensor", sAB, sAB.n)
            if DEBUG_STAGE == "abo":
                kb.wait("gpsimd", sAB, sAB.n)
                mk = kb.vop("engine_nop", sem=sV)
                dump_region(R2, sV, mk)
                return _finish(nc, kb)

            to_y = DEBUG_STAGE == 3
            proj(WobT, 512, D, lambda ic: ABO(ic),
                 mk_res_epi(scr2, yT if to_y else scr1, None if to_y else R1, BOB,
                            sY if to_y else sST),
                 rhs_ready=attnb_done)

        if DEBUG_STAGE == 0:
            kb.wait("vector", sS, sS.n)
            stats_chunks(R1)
            ln4 = ln_apply(R1, ln_finalize())
            ffn(False, R1, None, scr1, yT, sY, ln4)

        kb.wait("sync", sY, sY.n)
        return _finish(nc, kb)


_last_kb = None
def _finish(nc, kb):
    global _last_kb
    _last_kb = kb
    stuck = kb.check_feasible()
    if stuck:
        raise RuntimeError(f"static deadlock: {stuck[:5]}")
    with nc.Block() as block:
        @block.tensor
        def _(e):
            for f in kb.st["tensor"]:
                f(e)
        @block.vector
        def _(e):
            for f in kb.st["vector"]:
                f(e)
        @block.scalar
        def _(e):
            for f in kb.st["scalar"]:
                f(e)
        @block.gpsimd
        def _(e):
            for f in kb.st["gpsimd"]:
                f(e)
        @block.sync
        def _(e):
            for f in kb.st["sync"]:
                f(e)
    return nc


def _reshape_bias(v):
    v = np.ascontiguousarray(np.asarray(v, dtype=np.float32))
    n = v.shape[0]
    return np.ascontiguousarray(v.reshape(n // 128, 128).T)


def kernel(**inputs):
    x = np.asarray(inputs["L0_attn_a_hidden_states"], dtype=np.float32)
    f = lambda k: np.ascontiguousarray(np.asarray(inputs[k], dtype=np.float32))
    Wq_, Wk_, Wv_, Wo_ = f("L0_attn_a_Wq"), f("L0_attn_a_Wk"), f("L0_attn_a_Wv"), f("L0_attn_a_Wo")
    Wqk_w = f("L0_attn_b_Wqk_w")
    Wv_w = f("L0_attn_b_Wv_w")
    Wo_w = f("L0_attn_b_Wo_w")
    qweight = np.ascontiguousarray(np.asarray(inputs["L0_ffn_qweight"], dtype=np.int32))
    scales = f("L0_ffn_scales")
    zeros = f("L0_ffn_scaled_zeros")

    bq = _reshape_bias(inputs["L0_attn_a_bq"])
    bk = _reshape_bias(inputs["L0_attn_a_bk"])
    bv = _reshape_bias(inputs["L0_attn_a_bv"])
    bo = _reshape_bias(inputs["L0_attn_a_bo"])
    fb = _reshape_bias(inputs["L0_ffn_bias"])
    bob = _reshape_bias(inputs["L0_attn_b_Wo_b"])
    wqkb = np.asarray(inputs["L0_attn_b_Wqk_b"], dtype=np.float32)
    bqb = _reshape_bias(wqkb[0::2])
    bkb = _reshape_bias(wqkb[1::2])
    bvb = _reshape_bias(inputs["L0_attn_b_Wv_b"])
    bvecs = np.ascontiguousarray(
        np.concatenate([bq, bk, bv, bo, fb, bob, bqb, bkb, bvb], axis=1))

    shared = {
        "Wq": Wq_, "Wk": Wk_, "Wv": Wv_, "Wo": Wo_,
        "bvecs": bvecs,
        "gq_row": np.ascontiguousarray(f("L0_attn_a_gq").reshape(1, D)),
        "gk_row": np.ascontiguousarray(f("L0_attn_a_gk").reshape(1, D)),
        "qw2d": np.ascontiguousarray(qweight.reshape(D, D)),
        "scl_r": np.ascontiguousarray(scales[:, :, 0].reshape(NCH, 128, 32)),
        "zer_r": np.ascontiguousarray(zeros[:, :, 0].reshape(NCH, 128, 32)),
        "WqkqT": np.ascontiguousarray(Wqk_w[0::2].T),
        "WqkkT": np.ascontiguousarray(Wqk_w[1::2].T),
        "WvbT": np.ascontiguousarray(Wv_w.T),
        "WobT": np.ascontiguousarray(Wo_w.T),
        "idnf": np.eye(128, dtype=np.float32),
    }

    in_maps = []
    for c in range(8):
        b, q0 = c // 4, (c % 4) * T
        m = dict(shared)
        m["x0T"] = np.ascontiguousarray(x[b, q0:q0 + T, :].T)
        in_maps.append(m)

    nc = build_nc()
    res = run_bass_kernel_spmd(nc, in_maps, core_ids=list(range(8)))

    out = np.empty((2, 4096, D), dtype=np.float32)
    for c in range(8):
        b, q0 = c // 4, (c % 4) * T
        out[b, q0:q0 + T, :] = res.results[c]["yT"].T
    return out


# version 06
# speedup vs baseline: 1.2637x; 1.0840x over previous
import math
import contextlib
import numpy as np
import ml_dtypes
import concourse.bass as bass
import concourse.mybir as mybir
from concourse.bass_utils import run_bass_kernel_spmd

F32 = mybir.dt.float32
BF16 = mybir.dt.bfloat16
I32 = mybir.dt.int32
AF = mybir.ActivationFunctionType
ALU = mybir.AluOpType
AXX = mybir.AxisListType.X

D = 4096
T = 1024
NCH = 32
NH_A = 32
EPS_LN = 1e-5
EPS_RMS = 1e-6
CA_SHIFT = 20.0
ISQ_A = 1.0 / math.sqrt(128.0)
ISQ_B = 1.0 / math.sqrt(512.0)

DEBUG_STAGE = 0


class Sem:
    __slots__ = ("h", "n", "name")
    def __init__(self, h, name=""):
        self.h = h
        self.n = 0
        self.name = name


INTERP_SAFE = False


class KB:
    def __init__(self):
        self.st = {e: [] for e in ("tensor", "vector", "scalar", "gpsimd", "sync")}
        self.meta = {e: [] for e in ("tensor", "vector", "scalar", "gpsimd", "sync")}

    def emit(self, eng, fn, meta=None):
        self.st[eng].append(fn)
        self.meta[eng].append(meta or ("op",))

    def wait(self, eng, sem, val):
        if val > 0:
            self.emit(eng, lambda e, s=sem.h, v=val: e.wait_ge(s, v),
                      meta=("wait", sem.name, val))

    def check_feasible(self):
        pos = {e: 0 for e in self.meta}
        sems = {}
        progressed = True
        while progressed:
            progressed = False
            for e, ms in self.meta.items():
                while pos[e] < len(ms):
                    m = ms[pos[e]]
                    if m[0] == "wait" and sems.get(m[1], 0) < m[2]:
                        break
                    if m[0] == "inc":
                        sems[m[1]] = sems.get(m[1], 0) + m[2]
                    pos[e] += 1
                    progressed = True
        stuck = [(e, pos[e], self.meta[e][pos[e]])
                 for e in self.meta if pos[e] < len(self.meta[e])]
        return stuck or None

    def mm(self, out, lhsT, rhs, start, stop, sem=None):
        if sem is None:
            self.emit("tensor", lambda e, o=out, a=lhsT, b=rhs, s1=start, s2=stop:
                      e.matmul(o, a, b, start=s1, stop=s2))
            return None
        sem.n += 1
        self.emit("tensor", lambda e, o=out, a=lhsT, b=rhs, s1=start, s2=stop, s=sem.h:
                  e.matmul(o, a, b, start=s1, stop=s2).then_inc(s, 1),
                  meta=("inc", sem.name, 1))
        return sem.n

    def trans(self, out, in_, iden, sem=None):
        if sem is None:
            self.emit("tensor", lambda e, o=out, i=in_, d=iden: e.transpose(o, i, d))
            return None
        sem.n += 1
        self.emit("tensor", lambda e, o=out, i=in_, d=iden, s=sem.h:
                  e.transpose(o, i, d).then_inc(s, 1),
                  meta=("inc", sem.name, 1))
        return sem.n

    def dma(self, eng, out, in_, sem):
        if INTERP_SAFE and sem.n > 0:
            self.wait(eng, sem, sem.n)
        sem.n += 16
        self.emit(eng, lambda e, o=out, i=in_, s=sem.h:
                  e.dma_start(out=o, in_=i).then_inc(s, 16),
                  meta=("inc", sem.name, 16))
        return sem.n

    def act(self, out, in_, func, bias, scale, sem=None, accum=None):
        if sem is None:
            self.emit("scalar", lambda e, o=out, i=in_, f=func, b=bias, sc=scale, a=accum:
                      e.activation(o, i, f, bias=b, scale=sc, accum_out=a))
            return None
        sem.n += 1
        self.emit("scalar", lambda e, o=out, i=in_, f=func, b=bias, sc=scale, a=accum, s=sem.h:
                  e.activation(o, i, f, bias=b, scale=sc, accum_out=a).then_inc(s, 1),
                  meta=("inc", sem.name, 1))
        return sem.n

    def scopy(self, out, in_, sem=None, scale=1.0, bias=0.0):
        if sem is None:
            self.emit("scalar", lambda e, o=out, i=in_, b=bias, sc=scale:
                      e.activation(o, i, AF.Copy, bias=b, scale=sc))
            return None
        sem.n += 1
        self.emit("scalar", lambda e, o=out, i=in_, b=bias, sc=scale, s=sem.h:
                  e.activation(o, i, AF.Copy, bias=b, scale=sc).then_inc(s, 1),
                  meta=("inc", sem.name, 1))
        return sem.n

    def vop(self, name, *args, sem=None, **kw):
        if sem is None:
            self.emit("vector", lambda e, n=name, a=args, k=dict(kw): getattr(e, n)(*a, **k))
            return None
        sem.n += 1
        self.emit("vector", lambda e, n=name, a=args, k=dict(kw), s=sem.h:
                  getattr(e, n)(*a, **k).then_inc(s, 1),
                  meta=("inc", sem.name, 1))
        return sem.n

    def gop(self, name, *args, sem=None, **kw):
        if sem is None:
            self.emit("gpsimd", lambda e, n=name, a=args, k=dict(kw): getattr(e, n)(*a, **k))
            return None
        sem.n += 1
        self.emit("gpsimd", lambda e, n=name, a=args, k=dict(kw), s=sem.h:
                  getattr(e, n)(*a, **k).then_inc(s, 1),
                  meta=("inc", sem.name, 1))
        return sem.n

    def coll(self, kind, groups, ins, outs, sem):
        sem.n += 1
        self.emit("gpsimd", lambda e, kk=kind, g=groups, ii=ins, uu=outs, s=sem.h:
                  e.collective_compute(kk, ALU.bypass, replica_groups=g,
                                       ins=ii, outs=uu).then_inc(s),
                  meta=("inc", sem.name, 16))
        return sem.n


def build_nc():
    nc = bass.Bass("TRN2", target_bir_lowering=False)

    x0T = nc.dram_tensor("x0T", [D, T], F32, kind="ExternalInput").ap()
    Wq = nc.dram_tensor("Wq", [D, D], F32, kind="ExternalInput").ap()
    Wk = nc.dram_tensor("Wk", [D, D], F32, kind="ExternalInput").ap()
    Wv = nc.dram_tensor("Wv", [D, D], F32, kind="ExternalInput").ap()
    Wo = nc.dram_tensor("Wo", [D, D], F32, kind="ExternalInput").ap()
    bvecs = nc.dram_tensor("bvecs", [128, 6 * NCH + 12], F32, kind="ExternalInput").ap()
    gq_row = nc.dram_tensor("gq_row", [1, D], F32, kind="ExternalInput").ap()
    gk_row = nc.dram_tensor("gk_row", [1, D], F32, kind="ExternalInput").ap()
    qw2d = nc.dram_tensor("qw2d", [D, D], I32, kind="ExternalInput").ap()
    scl_r = nc.dram_tensor("scl_r", [NCH, 128, 32], F32, kind="ExternalInput").ap()
    zer_r = nc.dram_tensor("zer_r", [NCH, 128, 32], F32, kind="ExternalInput").ap()
    WqkqT = nc.dram_tensor("WqkqT", [D, 512], F32, kind="ExternalInput").ap()
    WqkkT = nc.dram_tensor("WqkkT", [D, 512], F32, kind="ExternalInput").ap()
    WvbT = nc.dram_tensor("WvbT", [D, 512], F32, kind="ExternalInput").ap()
    WobT = nc.dram_tensor("WobT", [512, D], F32, kind="ExternalInput").ap()
    idnf = nc.dram_tensor("idnf", [128, 128], F32, kind="ExternalInput").ap()

    yT = nc.dram_tensor("yT", [D, T], F32, kind="ExternalOutput").ap()

    scr1 = nc.dram_tensor("scr1", [D, T], F32).ap()
    scr2 = nc.dram_tensor("scr2", [D, T], F32).ap()
    wts = nc.dram_tensor("wts", [NCH * 128, D], BF16).ap()
    mxscr = nc.dram_tensor("mxscr", [8, 128], F32).ap()
    dnscr = nc.dram_tensor("dnscr", [1, 1024], F32).ap()

    ka_in = nc.dram_tensor("ka_in", [D, T], BF16)
    ka_out = nc.dram_tensor("ka_out", [4 * D, T], BF16)
    va_in = nc.dram_tensor("va_in", [NH_A * 8 * 128, 128], BF16)
    va_out = nc.dram_tensor("va_out", [4 * NH_A * 8 * 128, 128], BF16)
    kb_in = nc.dram_tensor("kb_in", [512, T], BF16)
    kb_out = nc.dram_tensor("kb_out", [4 * 512, T], BF16)
    vb_in = nc.dram_tensor("vb_in", [8 * T, 64], BF16)
    vb_out = nc.dram_tensor("vb_out", [4 * 8 * T, 64], BF16)
    groups = [[0, 1, 2, 3], [4, 5, 6, 7]]

    with contextlib.ExitStack() as stack:
        sbt = lambda name, shape, dt: stack.enter_context(nc.sbuf_tensor(name, shape, dt))
        R1 = sbt("R1", [128, NCH * T], BF16)
        R2 = sbt("R2", [128, NCH * T], BF16)
        WS = sbt("WS", [128, 8192], BF16)
        XT = sbt("XT", [128, 8192], BF16)
        WDF = sbt("WDF", [128, 1024], F32)
        STF = [sbt(f"STF{i}", [128, 512], F32) for i in range(2)]
        X0F = [sbt(f"X0F{i}", [128, 512], F32) for i in range(2)]
        BIAS = sbt("BIAS", [128, 6 * NCH + 12], F32)
        SCL = sbt("SCL", [128, 64], F32)
        F32R = sbt("F32R", [128, 3072], F32)
        RB = sbt("RB", [1, 5120], BF16)
        IDN = sbt("IDN", [128, 128], F32)
        ONEC = sbt("ONEC", [128, 1], BF16)
        ONERF = sbt("ONERF", [1, 128], BF16)
        ONEF32 = sbt("ONEF32", [1, 128], F32)
        NEGCA = sbt("NEGCA", [128, 1], F32)
        ZEROC = sbt("ZEROC", [128, 1], F32)
        EPSC = sbt("EPSC", [1, 2], F32)
        MXQ = sbt("MXQ", [128, 16], F32)
        MXT = sbt("MXT", [8, 128], F32)
        QTL = sbt("QTL", [65, 1024], BF16)
        QTL2 = sbt("QTL2", [65, 1024], BF16)

        Mrow = F32R[0:1, 0:1024]
        VArow = F32R[0:1, 1024:2048]
        TMProw = F32R[0:1, 2048:3072]
        RSTD = Mrow
        RCP = VArow
        DN0 = TMProw
        DN64 = F32R[64:65, 2048:3072]
        GAINS = RB[0:1, 0:4096]
        RBROW = RB[0:1, 4096:5120]

        pS0 = stack.enter_context(nc.psum_tensor("pS0", [128, 1024], F32))
        pS1 = stack.enter_context(nc.psum_tensor("pS1", [128, 1024], F32))
        pO = stack.enter_context(nc.psum_tensor("pO", [128, 1024], F32))
        pD = stack.enter_context(nc.psum_tensor("pD", [128, 1024], F32))

        mks = lambda name: Sem(stack.enter_context(nc.semaphore(name)), name)
        sW, sX, sST, sKV = mks("sW"), mks("sX"), mks("sST"), mks("sKV")
        sT, sV, sS, sG = mks("sT"), mks("sV"), mks("sS"), mks("sG")
        sC1, sC2, sC3, sC4 = mks("sC1"), mks("sC2"), mks("sC3"), mks("sC4")
        sY = mks("sY")
        sAB = mks("sAB")
        sKB = mks("sKB")
        sNR = mks("sNR")
        sNG = mks("sNG")
        sWA, sWB = mks("sWA"), mks("sWB")
        sR = [mks(f"sR{i}") for i in range(4)]

        kb = KB()

        def rch(region, ch):
            return region[:, ch * T:(ch + 1) * T]

        kb.gop("memset", ONEC[:, :], 1.0)
        kb.gop("memset", ONERF[:, :], 1.0)
        kb.gop("memset", ONEF32[:, :], 1.0)
        kb.gop("memset", NEGCA[:, :], -CA_SHIFT)
        kb.gop("memset", ZEROC[:, :], 0.0)
        kb.gop("memset", EPSC[0:1, 0:1], EPS_LN)
        kb.gop("memset", EPSC[0:1, 1:2], EPS_RMS)
        kb.gop("engine_nop", sem=sG)
        consts_g = sG.n
        kb.dma("sync", IDN[:, :], idnf[:, :], sX)
        kb.dma("sync", BIAS[:, :], bvecs[:, :], sX)
        consts_x = sX.n
        BQ = BIAS[:, 0 * NCH:1 * NCH]
        BK = BIAS[:, 1 * NCH:2 * NCH]
        BV = BIAS[:, 2 * NCH:3 * NCH]
        BO = BIAS[:, 3 * NCH:4 * NCH]
        FB = BIAS[:, 4 * NCH:5 * NCH]
        BOB = BIAS[:, 5 * NCH:6 * NCH]
        BQB = BIAS[:, 6 * NCH:6 * NCH + 4]
        BKB = BIAS[:, 6 * NCH + 4:6 * NCH + 8]
        BVB = BIAS[:, 6 * NCH + 8:6 * NCH + 12]

        xld = [kb.dma("gpsimd", rch(R1, ch), x0T[ch * 128:(ch + 1) * 128, :], sW)
               for ch in range(NCH)]
        kb.wait("vector", sG, consts_g)
        kb.wait("tensor", sG, consts_g)
        kb.wait("tensor", sX, consts_x)

        def stats_chunks(region, wait_ld=None):
            prev_mm = 0
            if wait_ld is not None:
                kb.wait("vector", sW, wait_ld[-1])
            for ch in range(NCH):
                kb.wait("vector", sT, prev_mm)
                vv = kb.vop("tensor_tensor", XT[:, 0:T], rch(region, ch),
                            rch(region, ch), op=ALU.mult, sem=sV)
                kb.wait("tensor", sV, vv)
                st, sp = ch == 0, ch == NCH - 1
                xc = rch(region, ch)
                kb.mm(pO[0:1, 0:512], ONEC[:, :], xc[:, 0:512], st, sp)
                kb.mm(pO[0:1, 512:1024], ONEC[:, :], xc[:, 512:1024], st, sp)
                kb.mm(pD[0:1, 0:512], ONEC[:, :], XT[:, 0:512], st, sp)
                prev_mm = kb.mm(pD[0:1, 512:1024], ONEC[:, :], XT[:, 512:1024], st, sp,
                                sem=sT)

        acc_free = {0: 0, 1: 0}
        stats_chunks(R1, wait_ld=xld)

        def ln_finalize():
            kb.wait("vector", sT, sT.n)
            va = kb.vop("tensor_scalar", Mrow, pO[0:1, :], 1.0 / D, None, op0=ALU.mult,
                        sem=sV)
            vb = kb.vop("tensor_scalar", VArow, pD[0:1, :], 1.0 / D, None, op0=ALU.mult,
                        sem=sV)
            kb.wait("vector", sV, vb)
            vc = kb.vop("tensor_tensor", TMProw, Mrow, Mrow, op=ALU.mult, sem=sV)
            kb.wait("vector", sV, vc)
            vv = kb.vop("tensor_tensor", VArow, VArow, TMProw, op=ALU.subtract, sem=sV)
            kb.wait("tensor", sV, sV.n)
            kb.mm(pS0[:, 0:512], ONEF32[0:1, :], Mrow[:, 0:512], True, True)
            tm = kb.mm(pS0[:, 512:1024], ONEF32[0:1, :], Mrow[:, 512:1024], True, True,
                       sem=sT)
            kb.wait("scalar", sV, vv)
            sl = kb.act(VArow, VArow, AF.Ln, EPSC[0:1, 0:1], 1.0, sem=sS)
            kb.wait("scalar", sS, sl)
            ss = kb.act(VArow, VArow, AF.Exp, ZEROC[0:1, 0:1], -0.5, sem=sS)
            kb.wait("scalar", sS, ss)
            kb.wait("scalar", sT, tm)
            sr = kb.scopy(RBROW, VArow, sem=sS)
            kb.wait("tensor", sS, sr)
            kb.mm(pS1[:, 0:512], ONERF[0:1, :], RBROW[:, 0:512], True, True)
            tt = kb.mm(pS1[:, 512:1024], ONERF[0:1, :], RBROW[:, 512:1024], True, True,
                       sem=sT)
            return tt

        def ln_apply(region, mr_ready):
            kb.wait("vector", sT, mr_ready)
            kb.wait("vector", sST, sST.n)
            kb.wait("vector", sY, sY.n)
            kb.wait("vector", sS, sS.n)
            for ch in range(NCH):
                xc = rch(region, ch)
                for tb in range(2):
                    hs = slice(tb * 512, (tb + 1) * 512)
                    ve = kb.vop("tensor_tensor", STF[tb][:, :], xc[:, hs],
                                pS0[:, hs], op=ALU.subtract, sem=sV)
                    kb.wait("vector", sV, ve)
                    vf = kb.vop("tensor_tensor", xc[:, hs], STF[tb][:, :],
                                pS1[:, hs], op=ALU.mult, sem=sV)
                    kb.wait("vector", sV, vf)
            m = kb.vop("engine_nop", sem=sV)
            acc_free[0] = m
            acc_free[1] = m
            return m

        def dump_region(region, marker_sem, marker):
            kb.wait("gpsimd", marker_sem, marker)
            for ch in range(NCH):
                kb.dma("gpsimd", yT[ch * 128:(ch + 1) * 128, :], rch(region, ch), sY)
            kb.wait("sync", sY, sY.n)

        if DEBUG_STAGE == "xb":
            mk = kb.vop("engine_nop", sem=sV)
            dump_region(R1, sV, mk)
            return _finish(nc, kb)
        ln1 = ln_apply(R1, ln_finalize())
        if DEBUG_STAGE == "ln1":
            dump_region(R1, sV, ln1)
            return _finish(nc, kb)

        def proj(W, n_in, n_out, rhs_ch, epi, rhs_ready=None):
            n_ic = n_in // 128
            n_oc = n_out // 128
            Wr = W.rearrange("(ic p) o -> p ic o", p=128)
            ldv, mmdone = {}, {}
            if rhs_ready:
                kb.wait("tensor", sV, rhs_ready)
            kb.wait("gpsimd", sT, sT.n)
            kb.wait("gpsimd", sST, sST.n)
            kb.wait("gpsimd", sAB, sAB.n)
            kb.wait("gpsimd", sNR, sNR.n)
            kb.wait("gpsimd", sNG, sNG.n)
            for oc in range(n_oc):
                slot = oc % 2
                wsem = sWA if slot == 0 else sWB
                if oc >= 2:
                    kb.wait("gpsimd", sT, mmdone[oc - 2])
                half = max(n_ic // 2, 1) if n_ic >= 8 else n_ic
                for g in range(n_ic // half):
                    kb.dma("gpsimd",
                           WS[:, slot * 4096 + g * half * 128:
                              slot * 4096 + (g + 1) * half * 128],
                           Wr[:, g * half:(g + 1) * half, oc * 128:(oc + 1) * 128], wsem)
                ldv[oc] = wsem.n
                kb.wait("tensor", wsem, ldv[oc])
                acc = pS0 if oc % 2 == 0 else pS1
                kb.wait("tensor", sV, acc_free[oc % 2])
                for tb in range(2):
                    for ic in range(n_ic):
                        kb.mm(acc[:, tb * 512:(tb + 1) * 512],
                              WS[:, slot * 4096 + ic * 128:slot * 4096 + (ic + 1) * 128],
                              rhs_ch(ic)[:, tb * 512:(tb + 1) * 512],
                              ic == 0, ic == n_ic - 1,
                              sem=(sT if (ic == n_ic - 1 and tb == 1) else None))
                mmdone[oc] = sT.n
                acc_free[oc % 2] = epi(oc, acc, sT.n)

        def mk_epi_raw(dstreg, bias_cols, stats):
            state = {"mm": 0}
            def epi(oc, acc, tA):
                kb.wait("vector", sT, tA)
                dst = rch(dstreg, oc)
                vv = kb.vop("tensor_scalar", dst, acc[:, :], bias_cols[:, oc:oc + 1],
                            None, op0=ALU.add, sem=sV)
                if not stats:
                    return vv
                kb.wait("vector", sT, state["mm"])
                kb.wait("vector", sV, vv)
                v2 = kb.vop("tensor_tensor", XT[:, 0:T], dst, dst, op=ALU.mult, sem=sV)
                kb.wait("tensor", sV, v2)
                st, sp = oc == 0, oc == NCH - 1
                kb.mm(pD[0:1, 0:512], ONEC[:, :], XT[:, 0:512], st, sp)
                state["mm"] = kb.mm(pD[0:1, 512:1024], ONEC[:, :], XT[:, 512:1024],
                                    st, sp, sem=sT)
                return v2
            return epi

        proj(Wk, D, D, lambda ic: rch(R1, ic), mk_epi_raw(R2, BK, True), rhs_ready=ln1)
        if DEBUG_STAGE == "kraw0":
            mk = kb.vop("engine_nop", sem=sV)
            dump_region(R2, sV, mk)
            return _finish(nc, kb)

        def rms_apply(grow_dram, out_cb):
            kb.wait("gpsimd", sT, sT.n)
            gld = kb.dma("gpsimd", GAINS, grow_dram[0:1, :], sNG)
            kb.wait("vector", sT, sT.n)
            vv = kb.vop("tensor_scalar", RSTD, pD[0:1, :], 1.0 / D, None,
                        op0=ALU.mult, sem=sV)
            kb.wait("scalar", sV, vv)
            s1_ = kb.act(RSTD, RSTD, AF.Ln, EPSC[0:1, 1:2], 1.0, sem=sS)
            kb.wait("scalar", sS, s1_)
            s2_ = kb.act(RSTD, RSTD, AF.Exp, ZEROC[0:1, 0:1], -0.5, sem=sS)
            kb.wait("scalar", sS, s2_)
            ss = kb.scopy(RBROW, RSTD, sem=sS)
            kb.wait("tensor", sS, ss)
            kb.wait("tensor", sNG, gld)
            kb.wait("tensor", sS, sS.n)
            gfree = {0: 0, 1: acc_free[0]}
            for oc in range(NCH):
                gb = oc % 2
                G = pO if gb == 0 else pS0
                kb.wait("tensor", sV, gfree[gb])
                kb.mm(G[:, 0:512], GAINS[0:1, oc * 128:(oc + 1) * 128], RBROW[:, 0:512],
                      True, True)
                tg = kb.mm(G[:, 512:1024], GAINS[0:1, oc * 128:(oc + 1) * 128],
                           RBROW[:, 512:1024], True, True, sem=sT)
                kb.wait("vector", sT, tg)
                gfree[gb] = out_cb(oc, G)
            acc_free[0] = max(acc_free[0], gfree[1])

        KH = [XT[:, 0:1024], XT[:, 1024:2048]]
        kexp = {}
        def k_out(oc, G):
            slot = oc % 2
            if oc >= 2:
                kb.wait("vector", sST, sST.n)
            vv = kb.vop("tensor_tensor", KH[slot], rch(R2, oc), G[:, :],
                        op=ALU.mult, sem=sV)
            kb.wait("sync", sV, vv)
            kexp[oc] = kb.dma("sync", ka_in.ap()[oc * 128:(oc + 1) * 128, :],
                              KH[slot], sST)
            return vv
        rms_apply(gk_row, k_out)
        kb.wait("gpsimd", sST, sST.n)
        kb.coll("AllGather", groups, [ka_in.ap().opt()], [ka_out.ap().opt()], sC1)
        kb.wait("gpsimd", sC1, 1)

        def v_epi_mk(dst_block):
            def epi(oc, acc, tA):
                kb.wait("vector", sT, tA)
                for tb in range(2):
                    vv = kb.vop("tensor_scalar", STF[tb][:, :],
                                acc[:, tb * 512:(tb + 1) * 512], BV[:, oc:oc + 1],
                                None, op0=ALU.add, sem=sV)
                    kb.wait("tensor", sV, vv)
                    kb.wait("tensor", sS, sS.n)
                    for j in range(4):
                        tc = tb * 4 + j
                        kb.trans(pO[:, tc * 128:(tc + 1) * 128], STF[tb][:, j * 128:(j + 1) * 128],
                                 IDN[:, :], sem=(sT if j == 3 else None))
                    kb.wait("scalar", sT, sT.n)
                    for j in range(4):
                        tc = tb * 4 + j
                        kb.scopy(dst_block(tc)[:, oc * 128:(oc + 1) * 128],
                                 pO[:, tc * 128:(tc + 1) * 128],
                                 sem=(sS if j == 3 else None))
                return sV.n
            return epi
        proj(Wv, D, D, lambda ic: rch(R1, ic),
             v_epi_mk(lambda tc: R2[:, tc * 4096:(tc + 1) * 4096]))
        va_view = va_in.ap().rearrange("(h tc t) d -> tc t h d", h=NH_A, tc=8)
        kb.wait("sync", sS, sS.n)
        for tc in range(8):
            kb.dma("sync", va_view[tc],
                   R2[:, tc * 4096:(tc + 1) * 4096].rearrange("p (h d) -> p h d", d=128),
                   sST)
        kb.wait("gpsimd", sST, sST.n)
        kb.coll("AllGather", groups, [va_in.ap().opt()], [va_out.ap().opt()], sC2)
        kb.wait("gpsimd", sC2, 1)

        if DEBUG_STAGE == "qbarrier":
            kb.wait("gpsimd", sC1, 1)
            kb.wait("gpsimd", sC2, 1)
            kb.wait("gpsimd", sST, sST.n)
        if DEBUG_STAGE == "qcoll":
            kb.wait("gpsimd", sC1, 1)
            kb.wait("gpsimd", sC2, 1)
        if DEBUG_STAGE == "qexp":
            kb.wait("gpsimd", sST, sST.n)
        proj(Wq, D, D, lambda ic: rch(R1, ic), mk_epi_raw(R2, BQ, True))
        if DEBUG_STAGE == "xhat_after":
            mk = kb.vop("engine_nop", sem=sV)
            dump_region(R1, sV, mk)
            return _finish(nc, kb)
        def q_out(oc, G):
            return kb.vop("tensor_tensor", rch(R1, oc), rch(R2, oc), G[:, :],
                          op=ALU.mult, sem=sV)
        rms_apply(gq_row, q_out)
        qhat_done = sV.n
        if DEBUG_STAGE in ("qbarrier", "qcoll", "qexp"):
            dump_region(R2, sV, qhat_done)
            return _finish(nc, kb)
        if DEBUG_STAGE == "qhat":
            dump_region(R1, sV, qhat_done)
            return _finish(nc, kb)
        if DEBUG_STAGE == "kraw":
            dump_region(R2, sV, qhat_done)
            return _finish(nc, kb)

        ka_out_ap = ka_out.ap()
        va_view_o = va_out.ap().rearrange("(r h tc t) d -> r h t tc d",
                                          r=4, h=NH_A, tc=8)
        kb.wait("sync", sC1, 1)
        kb.wait("sync", sC2, 1)
        kb.wait("sync", sT, sT.n)
        ldmark = {}
        uses = {}
        def load_rank(i):
            h, r = divmod(i, 4)
            s = i % 4
            kb.dma("sync", WS[:, s * 2048:s * 2048 + 1024],
                   ka_out_ap[r * D + h * 128:r * D + (h + 1) * 128, :], sR[s])
            ldmark[i] = kb.dma(
                "sync",
                WS[:, s * 2048 + 1024:s * 2048 + 2048].rearrange("p (tc d) -> p tc d", d=128),
                va_view_o[r, h], sR[s])
        load_rank(0)
        load_rank(1)
        kb.wait("tensor", sV, qhat_done)
        head_tail_v = qhat_done
        expc, pvc, tsd = {}, {}, {}
        pv_next = [0]

        def emit_pv_upto(Fmax):
            while pv_next[0] <= Fmax:
                emit_pv(pv_next[0])
                pv_next[0] += 1

        def emit_pv(F):
            hh, cc = divmod(F, NCH)
            rr, c8 = divmod(cc, 8)
            s = (hh * 4 + rr) % 4
            vbuf = WS[:, s * 2048 + 1024:s * 2048 + 2048]
            kb.wait("tensor", sS, expc[F])
            st, sp = cc == 0, cc == NCH - 1
            if st and hh > 0:
                kb.wait("tensor", sV, head_tail_v)
            pbuf = XT[:, (F % 4) * 1024:(F % 4 + 1) * 1024]
            kb.mm(pO[:, 0:512], vbuf[:, c8 * 128:(c8 + 1) * 128], pbuf[:, 0:512],
                  st, sp)
            kb.mm(pO[:, 512:1024], vbuf[:, c8 * 128:(c8 + 1) * 128],
                  pbuf[:, 512:1024], st, sp)
            kb.mm(pD[0:1, 0:512], ONEC[:, :], pbuf[:, 0:512], st, sp)
            pvc[F] = kb.mm(pD[0:1, 512:1024], ONEC[:, :], pbuf[:, 512:1024],
                           st, sp, sem=sT)
            if c8 == 7:
                uses[hh * 4 + rr] = pvc[F]

        for F in range(NH_A * NCH):
            h, c = divmod(F, NCH)
            r, c8 = divmod(c, 8)
            i = h * 4 + r
            s = i % 4
            Sp = pS0 if (c + h) % 2 == 0 else pS1
            if c8 == 0:
                if i + 2 < NH_A * 4:
                    if i >= 2:
                        kb.wait("sync", sT, uses[i - 2])
                    load_rank(i + 2)
                kb.wait("tensor", sR[s], ldmark[i])
            kbuf = WS[:, s * 2048:s * 2048 + 1024]
            if c >= 2:
                kb.wait("tensor", sS, expc[F - 2])
            elif h > 0:
                if c == 0:
                    kb.wait("tensor", sS, expc[F - 1])
                else:
                    kb.wait("tensor", sV, head_tail_v)
            kb.mm(Sp[:, 0:512], kbuf[:, c8 * 128:(c8 + 1) * 128],
                  rch(R1, h)[:, 0:512], True, True)
            tsd[F] = kb.mm(Sp[:, 512:1024], kbuf[:, c8 * 128:(c8 + 1) * 128],
                           rch(R1, h)[:, 512:1024], True, True, sem=sT)
            kb.wait("scalar", sT, tsd[F])
            if F >= 4:
                kb.wait("scalar", sT, pvc[F - 4])
            pbuf = XT[:, (F % 4) * 1024:(F % 4 + 1) * 1024]
            expc[F] = kb.act(pbuf, Sp[:, :], AF.Exp, NEGCA[:, :], ISQ_A, sem=sS)
            emit_pv_upto(F - 1)
            if c == NCH - 1:
                emit_pv_upto(F)
                kb.wait("vector", sT, pvc[F])
                vr = kb.vop("reciprocal", RCP, pD[0:1, :], sem=sV)
                kb.wait("vector", sV, vr)
                rb = kb.vop("tensor_copy", RBROW, RCP, sem=sV)
                kb.wait("scalar", sT, pvc[F])
                kb.wait("scalar", sV, head_tail_v)
                oc_cp = kb.scopy(XT[:, 4096:5120], pO[:, :], sem=sS)
                kb.wait("tensor", sV, rb)
                kb.wait("tensor", sS, expc[F - 1])
                SpE = pS0 if (30 + h) % 2 == 0 else pS1
                kb.mm(SpE[:, 0:512], ONERF[0:1, :], RBROW[:, 0:512], True, True)
                tb1 = kb.mm(SpE[:, 512:1024], ONERF[0:1, :], RBROW[:, 512:1024],
                            True, True, sem=sT)
                kb.wait("vector", sT, tb1)
                kb.wait("vector", sS, oc_cp)
                head_tail_v = kb.vop("tensor_tensor", rch(R2, h), XT[:, 4096:5120],
                                     SpE[:, :], op=ALU.mult, sem=sV)
        attn_done = sV.n
        acc_free[0] = head_tail_v
        acc_free[1] = head_tail_v
        if DEBUG_STAGE == "attn":
            dump_region(R2, sV, attn_done)
            return _finish(nc, kb)

        resid_ld = {}
        stf_free = [0, 0]
        x0f_free = [0, 0]
        stf_scpy = [0, 0]

        def mk_res_epi(src_ap, dst_ap, dst_reg, bias_cols, sto):
            def epi(oc, acc, tA):
                if oc == 0:
                    resid_ld.clear()
                    stf_free[0] = stf_free[1] = 0
                    kb.wait("vector", sST, sST.n)
                    kb.wait("vector", sY, sY.n)
                    kb.wait("vector", sS, sS.n)
                    stf_scpy[0] = stf_scpy[1] = 0
                    kb.wait("sync", sV, max(x0f_free))
                    kb.wait("sync", sST, sST.n)
                kb.wait("vector", sT, tA)
                for tb in range(2):
                    kb.wait("sync", sV, x0f_free[tb])
                    resid_ld[(oc, tb)] = kb.dma(
                        "sync", X0F[tb][:, :],
                        src_ap[oc * 128:(oc + 1) * 128, tb * 512:(tb + 1) * 512], sX)
                for tb in range(2):
                    kb.wait("vector", sX, resid_ld[(oc, 1)])
                    kb.wait("vector", sto, sto.n)
                    kb.wait("vector", sS, sS.n)
                    vv = kb.vop("scalar_tensor_tensor", STF[tb][:, :],
                                acc[:, tb * 512:(tb + 1) * 512],
                                bias_cols[:, oc:oc + 1], X0F[tb][:, :],
                                op0=ALU.add, op1=ALU.add, sem=sV)
                    x0f_free[tb] = vv
                    kb.wait("sync", sV, vv)
                    stf_free[tb] = kb.dma(
                        "sync", dst_ap[oc * 128:(oc + 1) * 128, tb * 512:(tb + 1) * 512],
                        STF[tb][:, :], sto)
                    if dst_reg is not None:
                        kb.wait("scalar", sV, vv)
                        stf_scpy[tb] = kb.scopy(
                            rch(dst_reg, oc)[:, tb * 512:(tb + 1) * 512],
                            STF[tb][:, :], sem=sS)
                return x0f_free[1]
            return epi

        to_y = DEBUG_STAGE == 1
        proj(Wo, D, D, lambda ic: rch(R2, ic),
             mk_res_epi(x0T, yT if to_y else scr1, None if to_y else R1, BO,
                        sY if to_y else sST),
             rhs_ready=attn_done)

        def ffn(first, rhs_reg, out_reg, src_ap, dst_ap, sto, rhs_ready):
            qwr = qw2d.rearrange("(oc p) i -> oc p i", p=128)
            epi = mk_res_epi(src_ap, dst_ap, out_reg, FB, sto)
            deq_used, mmdone, wt_st, wtld = {}, {}, {}, {}
            trdone, cpd = {}, {}
            if rhs_ready:
                kb.wait("tensor", sV, rhs_ready)
            kb.wait("gpsimd", sT, sT.n)
            kb.wait("gpsimd", sST, sST.n)
            kb.wait("scalar", sT, sT.n)
            kb.wait("scalar", sST, sST.n)
            for oc in range(NCH):
                if first:
                    if oc >= 1:
                        kb.wait("gpsimd", sV, deq_used[oc - 1])
                    kb.dma("gpsimd", WS[:, 0:2048], qwr[oc, :, 0:2048], sW)
                    kb.dma("gpsimd", WS[:, 2048:4096], qwr[oc, :, 2048:4096], sW)
                    kb.dma("gpsimd", SCL[:, 0:32], scl_r[oc], sW)
                    qldv = kb.dma("gpsimd", SCL[:, 32:64], zer_r[oc], sW)
                    kb.wait("vector", sW, qldv)
                    for w in range(8):
                        wh = (w % 2) * 512
                        g_t = trdone.get((oc, w - 2)) if w >= 2 else trdone.get((oc - 1, 6 + w))
                        if g_t:
                            kb.wait("vector", sT, g_t)
                        for j in range(4):
                            g = w * 4 + j
                            kb.vop("tensor_scalar", WDF[:, wh + j * 128:wh + (j + 1) * 128],
                                   WS[:, g * 128:(g + 1) * 128],
                                   SCL[:, g:g + 1], SCL[:, 32 + g:32 + g + 1],
                                   op0=ALU.mult, op1=ALU.add,
                                   sem=(sV if j == 3 else None))
                        dq = sV.n
                        kb.wait("tensor", sV, dq)
                        po_h = (w % 2) * 512
                        g_c = cpd.get((oc, w - 2)) if w >= 2 else cpd.get((oc - 1, 6 + w))
                        if g_c:
                            kb.wait("tensor", sS, g_c)
                        if oc >= 1 and w == 0:
                            kb.wait("scalar", sST, wt_st.get(oc - 1, 0))
                        for j in range(4):
                            kb.trans(pO[:, po_h + j * 128:po_h + (j + 1) * 128],
                                     WDF[:, wh + j * 128:wh + (j + 1) * 128],
                                     IDN[:, :], sem=(sT if j == 3 else None))
                        trdone[(oc, w)] = sT.n
                        kb.wait("scalar", sT, trdone[(oc, w)])
                        cpd[(oc, w)] = kb.scopy(WS[:, 4096 + w * 512:4096 + (w + 1) * 512],
                                                pO[:, po_h:po_h + 512], sem=sS)
                    deq_used[oc] = sV.n
                    wcopy = sS.n
                    wt = WS[:, 4096:8192]
                    kb.wait("sync", sS, wcopy)
                    wt_st[oc] = kb.dma("sync", wts[oc * 128:(oc + 1) * 128, :], wt, sST)
                    kb.wait("tensor", sS, wcopy)
                else:
                    slot = oc % 2
                    wsem = sKV if slot == 0 else sKB
                    if oc == 0:
                        kb.wait("scalar", sST, sST.n)
                    if oc >= 2:
                        kb.wait("scalar", sT, mmdone[oc - 2])
                    kb.dma("scalar", WS[:, slot * 4096:slot * 4096 + 4096],
                           wts[oc * 128:(oc + 1) * 128, :], wsem)
                    wtld[oc] = wsem.n
                    kb.wait("tensor", wsem, wtld[oc])
                    wt = WS[:, slot * 4096:slot * 4096 + 4096]
                acc = pS0 if oc % 2 == 0 else pS1
                kb.wait("tensor", sV, acc_free[oc % 2])
                for tb in range(2):
                    for ic in range(NCH):
                        kb.mm(acc[:, tb * 512:(tb + 1) * 512],
                              wt[:, ic * 128:(ic + 1) * 128],
                              rch(rhs_reg, ic)[:, tb * 512:(tb + 1) * 512],
                              ic == 0, ic == NCH - 1,
                              sem=(sT if (ic == NCH - 1 and tb == 1) else None))
                mmdone[oc] = sT.n
                acc_free[oc % 2] = epi(oc, acc, sT.n)

        if DEBUG_STAGE != 1:
            kb.wait("vector", sS, sS.n)
            if DEBUG_STAGE == "x1b":
                mk = kb.vop("engine_nop", sem=sV)
                kb.wait("gpsimd", sS, sS.n)
                dump_region(R1, sV, mk)
                return _finish(nc, kb)
            stats_chunks(R1)
            ln2 = ln_apply(R1, ln_finalize())
            if DEBUG_STAGE == "ln2":
                dump_region(R1, sV, ln2)
                return _finish(nc, kb)
            to_y = DEBUG_STAGE == 2
            ffn(True, R1, None if to_y else R2, scr1, yT if to_y else scr2,
                sY if to_y else sST, ln2)

        if DEBUG_STAGE in (0, 3, "abo"):
            kb.wait("vector", sS, sS.n)
            stats_chunks(R2)
            ln3 = ln_apply(R2, ln_finalize())

            QB = lambda ch: XT[:, 4096 + ch * T:4096 + (ch + 1) * T]
            ABO = lambda ch: rch(R2, ch)
            VBT = lambda tc: R1[:, tc * 512:(tc + 1) * 512]

            def qb_epi(oc, acc, tA):
                kb.wait("vector", sT, tA)
                return kb.vop("tensor_scalar", QB(oc), acc[:, :], BQB[:, oc:oc + 1],
                              None, op0=ALU.add, sem=sV)
            proj(WqkqT, D, 512, lambda ic: rch(R2, ic), qb_epi, rhs_ready=ln3)

            kexp_b = {}
            def kbp_epi(oc, acc, tA):
                slot = oc % 2
                kb.wait("vector", sT, tA)
                if oc >= 2:
                    kb.wait("vector", sST, sST.n)
                vv = kb.vop("tensor_scalar", KH[slot], acc[:, :],
                            BKB[:, oc:oc + 1], None, op0=ALU.add, sem=sV)
                kb.wait("sync", sV, vv)
                kexp_b[oc] = kb.dma("sync", kb_in.ap()[oc * 128:(oc + 1) * 128, :],
                                    KH[slot], sST)
                return vv
            proj(WqkkT, D, 512, lambda ic: rch(R2, ic), kbp_epi)
            kb.wait("gpsimd", sST, sST.n)
            kb.coll("AllGather", groups, [kb_in.ap().opt()], [kb_out.ap().opt()], sC3)
            kb.wait("gpsimd", sC3, 1)

            def vbp_epi(oc, acc, tA):
                kb.wait("vector", sT, tA)
                if oc < 2:
                    kb.wait("vector", sST, sST.n)
                    kb.wait("vector", sS, sS.n)
                for tb in range(2):
                    vv = kb.vop("tensor_scalar", STF[tb][:, :],
                                acc[:, tb * 512:(tb + 1) * 512], BVB[:, oc:oc + 1],
                                None, op0=ALU.add, sem=sV)
                    kb.wait("tensor", sV, vv)
                    kb.wait("tensor", sS, sS.n)
                    for j in range(4):
                        tc = tb * 4 + j
                        kb.trans(pO[:, tc * 128:(tc + 1) * 128],
                                 STF[tb][:, j * 128:(j + 1) * 128],
                                 IDN[:, :], sem=(sT if j == 3 else None))
                    kb.wait("scalar", sT, sT.n)
                    for j in range(4):
                        tc = tb * 4 + j
                        kb.scopy(VBT(tc)[:, oc * 128:(oc + 1) * 128],
                                 pO[:, tc * 128:(tc + 1) * 128],
                                 sem=(sS if j == 3 else None))
                return sV.n
            proj(WvbT, D, 512, lambda ic: rch(R2, ic), vbp_epi)
            vb_view = vb_in.ap().rearrange("(h tc t) d -> tc t h d", h=8, tc=8)
            kb.wait("sync", sS, sS.n)
            for tc in range(8):
                kb.dma("sync", vb_view[tc],
                       VBT(tc).rearrange("p (h d) -> p h d", d=64), sST)
            kb.wait("gpsimd", sST, sST.n)
            kb.coll("AllGather", groups, [vb_in.ap().opt()], [vb_out.ap().opt()], sC4)
            kb.wait("gpsimd", sC4, 1)

            kb.wait("scalar", sC3, 1)
            kb.wait("scalar", sC4, 1)
            kb.wait("scalar", sT, sT.n)
            kb.wait("gpsimd", sT, sT.n)
            kbo_ap = kb_out.ap()
            vbo_view = vb_out.ap().rearrange("(r h tc t) d -> r h t tc d",
                                             r=4, h=8, tc=8)
            kb.gop("memset", WS[64:65, 0:4096], 1.0)
            kb.gop("memset", R1[64:65, 0:4096], 1.0)
            for s in range(2):
                base = 4096 + s * 520
                kb.gop("memset", WS[:, base + 64:base + 520:65], 1.0)
            kb.gop("engine_nop", sem=sG)
            kb.wait("scalar", sG, sG.n)

            KTB = [WS[:, 0:4096], R1[:, 0:4096]]
            QTB = [QTL, QTL2]
            kldb = {}
            vldb = {}
            vsem = {}
            vuse = {}
            qcpd = {}
            smx_d = {}
            pv31 = {-1: 0, -2: 0}
            mx_hist = {"store": 0}
            def load_kt(h, guard):
                kb.wait("scalar", sT, guard)
                kt = KTB[h % 2]
                for r in range(4):
                    kb.dma("scalar", kt[0:64, r * 1024:(r + 1) * 1024],
                           kbo_ap[r * 512 + h * 64:r * 512 + (h + 1) * 64, :], sKB)
                kldb[h] = sKB.n
            def load_vt(h, r, guard):
                s = (h * 4 + r) % 2
                kb.wait("scalar", sT, guard)
                base = 4096 + s * 520
                dst = WS[:, base:base + 520].rearrange("p (tc d) -> p tc d", d=65)[:, :, 0:64]
                vldb[(h, r)] = kb.dma("scalar", dst, vbo_view[r, h], sR[s])
                vsem[(h, r)] = s
            def load_q(h, guard):
                qrows = QB(h // 2)[64 * (h % 2):64 * (h % 2) + 64, :]
                kb.wait("sync", sT, guard)
                kb.wait("sync", sV, qb_ready)
                qcpd[h] = kb.dma("sync", QTB[h % 2][0:64, :], qrows, sNR)

            qb_ready = sV.n
            mxg = {0: sV.n, 1: sV.n}
            ht_v = sV.n
            abo_marks = []
            REG = [pD[:, 0:512], pD[:, 512:1024]]

            def max_step(hh, m):
                r, qt, k2 = m // 16, (m % 16) // 2, m % 2
                if m == 0:
                    kb.wait("tensor", sKB, kldb[hh])
                    kb.wait("tensor", sNR, qcpd[hh])
                    kb.wait("tensor", sS, smx_d.get(hh - 1, 0))
                kb.wait("tensor", sV, mxg[m % 2])
                tsx = kb.mm(REG[m % 2], QTB[hh % 2][0:64, qt * 128:(qt + 1) * 128],
                            KTB[hh % 2][0:64, r * 1024 + k2 * 512:
                                         r * 1024 + (k2 + 1) * 512],
                            True, True, sem=sT)
                kb.wait("vector", sT, tsx)
                kb.vop("reduce_max", MXQ[:, 8 + k2:9 + k2], REG[m % 2], axis=AXX,
                       sem=sV)
                mxg[m % 2] = sV.n
                kb.wait("vector", sV, mxg[m % 2])
                if k2 == 1:
                    if r == 0:
                        vm = kb.vop("tensor_tensor", MXQ[:, qt:qt + 1], MXQ[:, 8:9],
                                    MXQ[:, 9:10], op=ALU.max, sem=sV)
                        kb.wait("vector", sV, vm)
                    else:
                        vm = kb.vop("tensor_tensor", MXQ[:, qt:qt + 1],
                                    MXQ[:, qt:qt + 1], MXQ[:, 8:9], op=ALU.max,
                                    sem=sV)
                        kb.wait("vector", sV, vm)
                        vm = kb.vop("tensor_tensor", MXQ[:, qt:qt + 1],
                                    MXQ[:, qt:qt + 1], MXQ[:, 9:10], op=ALU.max,
                                    sem=sV)
                        kb.wait("vector", sV, vm)
                if m == 63:
                    vn = kb.vop("tensor_scalar", MXQ[:, 0:8], MXQ[:, 0:8], -1.0,
                                None, op0=ALU.mult, sem=sV)
                    kb.wait("vector", sV, vn)
                    mx_last = sV.n
                    kb.wait("tensor", sV, mx_last)
                    tmx = kb.trans(pD[0:8, 0:128], MXQ[:, 0:8], IDN[:, :], sem=sT)
                    mxg[0] = mxg[1] = sV.n
                    kb.wait("scalar", sT, tmx)
                    kb.wait("scalar", sNR, mx_hist["store"])
                    smx_d[hh] = kb.scopy(MXT[:, :], pD[0:8, 0:128], sem=sS)
                    kb.wait("sync", sS, smx_d[hh])
                    mx_hist["store"] = kb.dma("sync", mxscr[:, :], MXT[:, :], sNR)
                    kb.wait("gpsimd", sNR, mx_hist["store"])
                    kb.wait("gpsimd", sT, pv31[hh - 2])
                    qtl = kb.dma("gpsimd", QTB[hh % 2][64:65, :],
                                 mxscr.rearrange("a b -> (a b)")[None, :], sNG)
                    kb.wait("tensor", sNG, qtl)

            load_kt(0, 0)
            load_q(0, 0)
            for m in range(64):
                max_step(0, m)
            prev_expc = {}
            for h in range(8):
                if h + 1 < 8:
                    load_kt(h + 1, pv31[h - 1])
                    load_q(h + 1, pv31[h - 1])
                load_vt(h, 0, vuse.get((h - 1, 2), 0))
                load_vt(h, 1, vuse.get((h - 1, 3), 0))
                kt = KTB[h % 2]
                qt_b = QTB[h % 2]
                expc, pvc = {}, {}
                pvq = [0]
                def emit_pvb(c, h=h, expc=expc, pvc=pvc):
                    r, c8 = c // 8, c % 8
                    s = vsem[(h, r)]
                    vtb = WS[:, 4096 + s * 520:4096 + (s + 1) * 520]
                    if c8 == 0:
                        kb.wait("tensor", sR[s], vldb[(h, r)])
                    kb.wait("tensor", sS, expc[c])
                    st, sp = c == 0, c == 31
                    if st:
                        kb.wait("tensor", sV, ht_v)
                    pbuf = XT[:, (c % 3) * 1024:(c % 3 + 1) * 1024]
                    kb.mm(pO[0:65, 0:512], vtb[:, c8 * 65:(c8 + 1) * 65],
                          pbuf[:, 0:512], st, sp)
                    pvc[c] = kb.mm(pO[0:65, 512:1024], vtb[:, c8 * 65:(c8 + 1) * 65],
                                   pbuf[:, 512:1024], st, sp, sem=sT)
                    if c8 == 7:
                        vuse[(h, r)] = pvc[c]
                def pv_upto(cm):
                    while pvq[0] <= cm:
                        emit_pvb(pvq[0])
                        pvq[0] += 1
                for c in range(32):
                    r, c8 = c // 8, c % 8
                    bank = (c + h) % 2
                    Sp = pS0 if bank == 0 else pS1
                    if c >= 2:
                        kb.wait("tensor", sS, expc[c - 2])
                    elif h > 0:
                        kb.wait("tensor", sS, prev_expc[31 if c == 0 else 30])
                    else:
                        kb.wait("tensor", sV, acc_free[bank])
                    kb.mm(Sp[:, 0:512], kt[0:65, c * 128:(c + 1) * 128],
                          qt_b[0:65, 0:512], True, True)
                    ts = kb.mm(Sp[:, 512:1024], kt[0:65, c * 128:(c + 1) * 128],
                               qt_b[0:65, 512:1024], True, True, sem=sT)
                    kb.wait("scalar", sT, ts)
                    if c >= 3:
                        kb.wait("scalar", sT, pvc[c - 3])
                    if c == 2:
                        kb.wait("scalar", sV, ht_v)
                    pbuf = XT[:, (c % 3) * 1024:(c % 3 + 1) * 1024]
                    expc[c] = kb.act(pbuf, Sp[:, :], AF.Exp, ZEROC[:, :], 1.0, sem=sS)
                    pv_upto(c - 1)
                    if c8 == 0 and 1 <= r < 3:
                        load_vt(h, r + 1, vuse[(h, r - 1)])
                    if h + 1 < 8:
                        max_step(h + 1, 2 * c)
                        max_step(h + 1, 2 * c + 1)
                    if c == 31:
                        pv_upto(31)
                pv31[h] = pvc[31]
                prev_expc = expc
                kb.wait("scalar", sT, pvc[31])
                kb.wait("scalar", sV, ht_v)
                dcp = kb.scopy(DN64, pO[64:65, :], sem=sS)
                ht_s = kb.scopy(XT[0:64, 2 * 1024:3 * 1024], pO[0:64, :], sem=sS)
                kb.wait("sync", sS, dcp)
                dst1 = kb.dma("sync", dnscr[0:1, :], DN64, sNR)
                kb.wait("sync", sNR, dst1)
                dld = kb.dma("sync", DN0, dnscr[0:1, :], sNR)
                kb.wait("vector", sNR, dld)
                vr = kb.vop("reciprocal", RCP, DN0, sem=sV)
                kb.wait("scalar", sV, vr)
                srb = kb.scopy(RBROW, RCP, scale=ISQ_B, sem=sS)
                kb.wait("tensor", sS, srb)
                kb.wait("tensor", sS, ht_s)
                kb.mm(pO[0:64, 0:512], ONERF[0:1, 0:64], RBROW[:, 0:512], True, True)
                tbb = kb.mm(pO[0:64, 512:1024], ONERF[0:1, 0:64], RBROW[:, 512:1024],
                            True, True, sem=sT)
                kb.wait("vector", sT, tbb)
                kb.wait("vector", sS, ht_s)
                kb.wait("vector", sAB, abo_marks[-1] if abo_marks else 0)
                ht_v = kb.vop("tensor_tensor", XT[0:64, 3 * 1024:4 * 1024],
                              XT[0:64, 2 * 1024:3 * 1024],
                              pO[0:64, :], op=ALU.mult, sem=sV)
                kb.wait("sync", sV, ht_v)
                abo_dma = kb.dma("sync",
                                 R2[64 * (h % 2):64 * (h % 2) + 64,
                                    (h // 2) * T:(h // 2) * T + T],
                                 XT[0:64, 3 * 1024:4 * 1024], sAB)
                abo_marks.append(abo_dma)
            attnb_done = sV.n
            kb.wait("tensor", sS, sS.n)
            acc_free[0] = attnb_done
            acc_free[1] = attnb_done
            kb.wait("tensor", sAB, sAB.n)
            if DEBUG_STAGE == "abo":
                kb.wait("gpsimd", sAB, sAB.n)
                mk = kb.vop("engine_nop", sem=sV)
                dump_region(R2, sV, mk)
                return _finish(nc, kb)

            to_y = DEBUG_STAGE == 3
            proj(WobT, 512, D, lambda ic: ABO(ic),
                 mk_res_epi(scr2, yT if to_y else scr1, None if to_y else R1, BOB,
                            sY if to_y else sST),
                 rhs_ready=attnb_done)

        if DEBUG_STAGE == 0:
            kb.wait("vector", sS, sS.n)
            stats_chunks(R1)
            ln4 = ln_apply(R1, ln_finalize())
            ffn(False, R1, None, scr1, yT, sY, ln4)

        kb.wait("sync", sY, sY.n)
        return _finish(nc, kb)


_last_kb = None
def _finish(nc, kb):
    global _last_kb
    _last_kb = kb
    stuck = kb.check_feasible()
    if stuck:
        raise RuntimeError(f"static deadlock: {stuck[:5]}")
    with nc.Block() as block:
        @block.tensor
        def _(e):
            for f in kb.st["tensor"]:
                f(e)
        @block.vector
        def _(e):
            for f in kb.st["vector"]:
                f(e)
        @block.scalar
        def _(e):
            for f in kb.st["scalar"]:
                f(e)
        @block.gpsimd
        def _(e):
            for f in kb.st["gpsimd"]:
                f(e)
        @block.sync
        def _(e):
            for f in kb.st["sync"]:
                f(e)
    return nc


def _reshape_bias(v):
    v = np.ascontiguousarray(np.asarray(v, dtype=np.float32))
    n = v.shape[0]
    return np.ascontiguousarray(v.reshape(n // 128, 128).T)


def kernel(**inputs):
    x = np.asarray(inputs["L0_attn_a_hidden_states"], dtype=np.float32)
    f = lambda k: np.ascontiguousarray(np.asarray(inputs[k], dtype=np.float32))
    Wq_, Wk_, Wv_, Wo_ = f("L0_attn_a_Wq"), f("L0_attn_a_Wk"), f("L0_attn_a_Wv"), f("L0_attn_a_Wo")
    Wqk_w = f("L0_attn_b_Wqk_w")
    Wv_w = f("L0_attn_b_Wv_w")
    Wo_w = f("L0_attn_b_Wo_w")
    qweight = np.ascontiguousarray(np.asarray(inputs["L0_ffn_qweight"], dtype=np.int32))
    scales = f("L0_ffn_scales")
    zeros = f("L0_ffn_scaled_zeros")

    bq = _reshape_bias(inputs["L0_attn_a_bq"])
    bk = _reshape_bias(inputs["L0_attn_a_bk"])
    bv = _reshape_bias(inputs["L0_attn_a_bv"])
    bo = _reshape_bias(inputs["L0_attn_a_bo"])
    fb = _reshape_bias(inputs["L0_ffn_bias"])
    bob = _reshape_bias(inputs["L0_attn_b_Wo_b"])
    wqkb = np.asarray(inputs["L0_attn_b_Wqk_b"], dtype=np.float32)
    bqb = _reshape_bias(wqkb[0::2])
    bkb = _reshape_bias(wqkb[1::2])
    bvb = _reshape_bias(inputs["L0_attn_b_Wv_b"])
    bvecs = np.ascontiguousarray(
        np.concatenate([bq, bk, bv, bo, fb, bob, bqb, bkb, bvb], axis=1))

    shared = {
        "Wq": Wq_, "Wk": Wk_, "Wv": Wv_, "Wo": Wo_,
        "bvecs": bvecs,
        "gq_row": np.ascontiguousarray(f("L0_attn_a_gq").reshape(1, D)),
        "gk_row": np.ascontiguousarray(f("L0_attn_a_gk").reshape(1, D)),
        "qw2d": np.ascontiguousarray(qweight.reshape(D, D)),
        "scl_r": np.ascontiguousarray(scales[:, :, 0].reshape(NCH, 128, 32)),
        "zer_r": np.ascontiguousarray(zeros[:, :, 0].reshape(NCH, 128, 32)),
        "WqkqT": np.ascontiguousarray(Wqk_w[0::2].T),
        "WqkkT": np.ascontiguousarray(Wqk_w[1::2].T),
        "WvbT": np.ascontiguousarray(Wv_w.T),
        "WobT": np.ascontiguousarray(Wo_w.T),
        "idnf": np.eye(128, dtype=np.float32),
    }

    in_maps = []
    for c in range(8):
        b, q0 = c // 4, (c % 4) * T
        m = dict(shared)
        m["x0T"] = np.ascontiguousarray(x[b, q0:q0 + T, :].T)
        in_maps.append(m)

    nc = build_nc()
    res = run_bass_kernel_spmd(nc, in_maps, core_ids=list(range(8)))

    out = np.empty((2, 4096, D), dtype=np.float32)
    for c in range(8):
        b, q0 = c // 4, (c % 4) * T
        out[b, q0:q0 + T, :] = res.results[c]["yT"].T
    return out
